# version 03
# baseline (speedup 1.0000x reference)
import numpy as np
import ml_dtypes
import concourse.bass as bass
import concourse.mybir as mybir
from concourse.bass_utils import run_bass_kernel_spmd

F32 = mybir.dt.float32
BF16 = mybir.dt.bfloat16
I32 = mybir.dt.int32
AF = mybir.ActivationFunctionType
ALU = mybir.AluOpType
AX = mybir.AxisListType
BFnp = ml_dtypes.bfloat16

T = 4096
D = 1024
NDC = D // 128
SUP = 512
NSUP = T // SUP
CH = 128
NCHS = SUP // CH
NCH = T // CH
NH = 16
DH = 64
ENGS = ("sync", "tensor", "vector", "scalar", "gpsimd")
RG_PAIRS = [[0, 1], [2, 3], [4, 5], [6, 7]]
DBG = {"nsup_state": NSUP, "nsup_y": NSUP, "skip_exchange": False, "nchs": NCHS, "skip_ytail": False, "salt": 0}

M_DIN = 2048
M_NH = 32
M_DH = 64
M_DS = 128
M_CONV = 2304
M_PROJ = 4384
N_EXP = 32
TOPK = 6
EI = 512
IS = 1024
FH = 3072


class Ctx:
    def __init__(self, nc):
        self.nc = nc
        self.q = {e: [] for e in ENGS}
        self.sems = {}
        self.cnt = {}
        self.nph = 0
        self.opcount = 0

    def sem(self, name):
        if name not in self.sems:
            self.sems[name] = self.nc.alloc_semaphore(name)
            self.cnt[name] = 0
        return self.sems[name]

    def val(self, name):
        return self.cnt.get(name, 0)

    def allv(self, *names):
        return [(n, self.val(n)) for n in names if self.val(n) > 0]

    AUTO = {"vector": "V", "scalar": "S", "gpsimd": "P"}
    DMA_SEMS = ("G", "L", "CC")

    def op(self, e, fn, waits=(), incs=()):
        waits = list(waits)
        incs = list(incs)
        if e in self.AUTO:
            a = self.AUTO[e]
            self.sem(a)
            waits.append((a, self.cnt[a]))
            is_dma = any(n in self.DMA_SEMS for (n, _) in incs)
            if not is_dma and not any(n == a for (n, _) in incs):
                incs.append((a, 1))
        planned = []
        for (s, v) in incs:
            self.sem(s)
            self.cnt[s] += v
            planned.append((s, v))
        for (s, v) in waits:
            self.sem(s)
        self.q[e].append((fn, tuple(waits), tuple(planned)))
        self.opcount += 1

    def barrier(self):
        own = {"sync": ("L",), "tensor": ("T",), "vector": ("V",),
               "scalar": ("S",), "gpsimd": ("G", "P", "CC")}
        self.sem("bar")
        self.nph += 1
        tgt = 5 * self.nph
        for e in ENGS:
            dw = tuple((s, self.val(s)) for s in own[e] if self.val(s) > 0)
            def fn(eng, dw=dw, tgt=tgt, ctx=self):
                for (s, v) in dw:
                    eng.wait_ge(ctx.sems[s], v)
                eng.sem_inc(ctx.sems["bar"], 1)
                eng.wait_ge(ctx.sems["bar"], tgt)
                return None
            self.q[e].append((fn, (), ()))
        self.cnt["bar"] = tgt

    def run(self):
        with self.nc.Block() as block:
            def runner(ename):
                def _(eng):
                    for fn, waits, incs in self.q[ename]:
                        for (s, v) in waits:
                            if v > 0:
                                eng.wait_ge(self.sems[s], v)
                        ins = fn(eng)
                        for (s, v) in incs:
                            ins.then_inc(self.sems[s], v)
                return _
            block.sync(runner("sync"))
            block.tensor(runner("tensor"))
            block.vector(runner("vector"))
            block.scalar(runner("scalar"))
            block.gpsimd(runner("gpsimd"))


class Arena:
    def __init__(self, nc, nf32, nbf16):
        self.f = nc.alloc_sbuf_tensor("ARF", [128, nf32], F32)
        self.b = nc.alloc_sbuf_tensor("ARB", [128, nbf16], BF16)
        self.nf32, self.nbf16 = nf32, nbf16
        self.of = 0
        self.ob = 0
        self.peak = (0, 0)

    def fbuf(self, n, p=128):
        n = (n + 15) & ~15
        assert self.of + n <= self.nf32, f"f32 arena overflow at {self.of}+{n}"
        ap = self.f[0:p, self.of:self.of + n]
        self.of += n
        self.peak = (max(self.peak[0], self.of), self.peak[1])
        return ap

    def bbuf(self, n, p=128):
        n = (n + 15) & ~15
        assert self.ob + n <= self.nbf16, f"bf16 arena overflow at {self.ob}+{n}"
        ap = self.b[0:p, self.ob:self.ob + n]
        self.ob += n
        self.peak = (self.peak[0], max(self.peak[1], self.ob))
        return ap

    def mark(self):
        return (self.of, self.ob)

    def release(self, m):
        self.of, self.ob = m


def ceil128(x):
    return (x + 127) & ~127


class MK:
    def __init__(self, mode="fused", taps=()):
        self.mode = mode
        self.taps = set(taps)
        nc = bass.Bass("TRN2", target_bir_lowering=False)
        self.nc = nc
        self.ctx = Ctx(nc)
        self.ht = {}
        self.htkind = {}
        self.ar = Arena(nc, 21504, 62464)
        self.PS = [nc.alloc_psum_tensor(f"PS{i}", [128, 512], F32) for i in range(6)]
        self.PB = [nc.alloc_psum_tensor(f"PB{i}", [128, 1024], BF16) for i in range(2)]

    def hbm(self, name, shape, dtype, kind):
        if name in self.ht:
            return self.ht[name]
        if name in self.taps and kind == "scratch":
            kind = "out"
        k = {"in": "ExternalInput", "out": "ExternalOutput", "scratch": "Internal"}[kind]
        ap = self.nc.dram_tensor(name, list(shape), dtype, kind=k).ap()
        self.ht[name] = ap
        self.htkind[name] = kind
        return ap

    def stream(self, name, group_kind):
        kind = "scratch" if self.mode == "fused" else group_kind
        dt = BF16 if name.endswith("b") else F32
        return self.hbm(name, (D, T), dt, kind)

    def inline(self, name, arr):
        if name not in self.ht:
            self.ht[name] = self.nc.inline_tensor(np.ascontiguousarray(arr), name).ap()
            self.htkind[name] = "inline"
        return self.ht[name]


def ph_consts(mk: MK):
    c, ar = mk.ctx, mk.ar
    maskji = np.triu(np.ones((128, 128), np.float32))
    loads = []
    mk.ID_B = ar.bbuf(128)
    loads.append((mk.ID_B, mk.inline("c_ident", np.eye(128, dtype=BFnp))))
    mk.ID_F = ar.fbuf(128)
    loads.append((mk.ID_F, mk.inline("c_identf", np.eye(128, dtype=np.float32))))
    mk.MASK = ar.bbuf(128)
    loads.append((mk.MASK, mk.inline("c_mask", maskji.astype(BFnp))))
    mk.CUMF = ar.fbuf(128)
    loads.append((mk.CUMF, mk.inline("c_cumf", maskji)))
    mk.TRILX = ar.fbuf(32, p=32)
    loads.append((mk.TRILX, mk.inline("c_trilx", np.triu(np.ones((32, 32), np.float32), 1))))
    mk.ONES1B = ar.bbuf(512, p=1)
    loads.append((mk.ONES1B, mk.inline("c_ones1b", np.ones((1, 512), BFnp))))
    mk.ONES1F = ar.fbuf(512, p=65)
    loads.append((mk.ONES1F[0:1, :], mk.inline("c_ones1f", np.ones((1, 512), np.float32))))
    mk.ONESCB = ar.bbuf(16)
    loads.append((mk.ONESCB[:, 0:1], mk.inline("c_onescolb", np.ones((128, 1), BFnp))))
    mk.GATE = ar.fbuf(16)
    loads.append((mk.GATE[:, 0:1], mk.hbm("gate", (128, 1), F32, "in")))
    for dst, src in loads:
        c.op("sync", lambda s, d=dst, sr=src: s.dma_start(out=d, in_=sr), incs=[("L", 16)])
    mk.ECOLS = ar.fbuf(16)
    c.op("vector", lambda v_: v_.memset(mk.ONES1F[64:65, :], 1.0))
    for _ in range(DBG["salt"]):
        c.op("vector", lambda v_: v_.memset(mk.ECOLS[:, 15:16], 0.0))
    for col, v in [(0, 1e-6), (1, 1e-5), (2, 1e-12), (3, 1.0 / D), (4, 1.0 / M_DIN)]:
        c.op("vector", lambda v_, cl=col, vv=v: v_.memset(mk.ECOLS[:, cl:cl + 1], vv),
             incs=[("V", 1)] if col == 4 else ())
    c.barrier()


PREP = {
    "wq": ("L0_attn_Wq", D, D), "wk": ("L0_attn_Wk", D, D),
    "wv": ("L0_attn_Wv", D, D), "wo": ("L0_attn_Wo", D, D),
    "sgw": ("L0_ffn_shared_gate_w", IS, D), "suw": ("L0_ffn_shared_up_w", IS, D),
    "sdw": ("L0_ffn_shared_down_w", D, IS),
    "gup": ("L0_ffn_gate_up_proj", N_EXP * 2 * EI, D),
    "dwn": ("L0_ffn_down_proj", N_EXP * D, EI),
    "ipw": ("L1_attn_in_proj_w", M_PROJ, D),
    "opw": ("L1_attn_out_proj_w", D, M_DIN),
    "wg1": ("L1_ffn_w_gate", FH, D), "wu1": ("L1_ffn_w_up", FH, D),
    "wd1": ("L1_ffn_w_down", D, FH),
}
GROUP_W = {
    "g1": ["wq", "wk", "wv", "wo"],
    "g2": ["sgw", "suw", "sdw", "gup", "dwn"],
    "g3": ["ipw", "opw"],
    "g4": ["wg1", "wu1", "wd1"],
}
IN_DTYPE = {"L1_ffn_w_gate": I32, "L1_ffn_w_up": I32, "L1_ffn_w_down": I32}
IN_SHAPE3 = {"L0_ffn_gate_up_proj": (N_EXP, 2 * EI, D), "L0_ffn_down_proj": (N_EXP, D, EI)}


def ph_prep(mk: MK, names):
    c = mk.ctx
    for nm in names:
        src_name, rows, cols = PREP[nm]
        prows = ceil128(rows)
        shp = IN_SHAPE3.get(src_name, (rows, cols))
        src = mk.hbm("i_" + src_name, shp, IN_DTYPE.get(src_name, F32), "in")
        scr = mk.hbm("scr_" + nm, (prows, cols), BF16, "scratch")
        sap = src if len(shp) == 2 else src.rearrange("a b c -> (a b) c")
        r0 = 0
        while r0 < rows:
            r1 = min(rows, r0 + 2048)
            c.op("gpsimd",
                 lambda g, o=scr[r0:r1, :], i=sap[r0:r1, :]: g.dma_start(out=o, in_=i),
                 incs=[("G", 16)])
            r0 = r1
    c.barrier()


def load_wT(mk, c, scr_name, in_dim, out_rows, dst, row0=0):
    scr = mk.ht["scr_" + scr_name]
    for dc in range(in_dim // 128):
        c.op("sync",
             lambda s, o=dst[:, dc * out_rows:(dc + 1) * out_rows],
                    i=scr[row0:row0 + out_rows, dc * 128:(dc + 1) * 128]:
             s.dma_start(out=o, in_=i, transpose=True),
             incs=[("L", 16)])


def load_row(mk, c, input_name, n, dst_row, cast=True):
    src = mk.hbm("i_" + input_name, (n,), F32, "in")
    eng = "gpsimd" if cast else "sync"
    inc = ("G", 16) if cast else ("L", 16)
    c.op(eng, lambda g, o=dst_row, i=src.unsqueeze(0): g.dma_start(out=o, in_=i),
         incs=[inc])


def ln_fm(mk, c, zfn, W, out_b16_fn, psS, psA, psB2):
    zb, sq, rows, abc = mk.LN_ZB, mk.LN_SQ, mk.LN_ROWS, mk.LN_ABC
    for dc in range(NDC):
        c.op("vector", lambda v, dc_=dc: v.tensor_copy(zb[:, dc_ * W:(dc_ + 1) * W], zfn(dc_)),
             waits=(c.allv("T") if dc == 0 else ()),
             incs=([("V", 1)] if dc == NDC - 1 else ()))
    vcp = c.val("V")
    c.op("scalar", lambda s: s.activation(sq[:, 0:NDC * W], zb[:, 0:NDC * W], AF.Square),
         waits=[("V", vcp)], incs=[("S", 1)])
    ssq = c.val("S")
    for j, src in ((0, zb), (1, sq)):
        for dc in range(NDC):
            c.op("tensor",
                 lambda t, dcc=dc, jj=j, sr=src:
                 t.matmul(psS[32 * jj:32 * jj + 1, 0:W], mk.ONESCB[:, 0:1],
                          sr[:, dcc * W:(dcc + 1) * W],
                          start=(dcc == 0), stop=(jj == 1 and dcc == NDC - 1),
                          skip_group_check=True,
                          tile_position=(0, 32) if jj == 1 else None),
                 waits=([("V", vcp)] if (j == 0 and dc == 0) else
                        ([("S", ssq)] if (j == 1 and dc == 0) else ())),
                 incs=[("T", 1)] if (j == 1 and dc == NDC - 1) else ())
    tmm = c.val("T")
    mean, ex2, m2, var, rr, bb = (rows[0:1, k * W:(k + 1) * W] for k in range(6))
    c.op("vector", lambda v: v.tensor_scalar(mean, psS[0:1, 0:W], 1.0 / D, None, ALU.mult),
         waits=[("T", tmm)])
    c.op("vector", lambda v: v.tensor_scalar(ex2, psS[32:33, 0:W], 1.0 / D, None, ALU.mult))
    c.op("vector", lambda v: v.tensor_tensor(m2, mean, mean, ALU.mult))
    c.op("vector", lambda v: v.scalar_tensor_tensor(var, m2, -1.0, ex2, ALU.mult, ALU.add),
         incs=[("V", 1)])
    vvar = c.val("V")
    c.op("scalar", lambda s: s.activation(var, var, AF.Ln, bias=mk.ECOLS[0:1, 1:2]),
         waits=[("V", vvar)])
    c.op("scalar", lambda s: s.activation(rr, var, AF.Exp, scale=-0.5), incs=[("S", 1)])
    srr = c.val("S")
    c.op("vector", lambda v: v.scalar_tensor_tensor(bb, mean, -1.0, rr, ALU.mult, ALU.mult),
         waits=[("S", srr)], incs=[("V", 1)])
    vbb = c.val("V")
    c.op("tensor", lambda t: t.matmul(psA[:, 0:W], mk.ONES1F[0:1, 0:128], rr,
                                      start=True, stop=True, skip_group_check=True),
         waits=[("V", vbb)])
    c.op("tensor", lambda t: t.matmul(psB2[:, 0:W], mk.ONES1F[0:1, 0:128], bb,
                                      start=True, stop=True, skip_group_check=True),
         incs=[("T", 1)])
    tbc = c.val("T")
    c.op("vector", lambda v: v.tensor_copy(abc[:, 0:W], psA[:, 0:W]),
         waits=[("T", tbc)] + c.allv("P"))
    c.op("vector", lambda v: v.tensor_copy(abc[:, W:2 * W], psB2[:, 0:W]), incs=[("V", 1)])
    vab = c.val("V")
    for dc in range(NDC):
        of = zfn(dc)
        ob = out_b16_fn(dc)
        c.op("gpsimd",
             lambda g, of_=of: g.tensor_tensor(of_, of_, abc[:, 0:W], ALU.mult),
             waits=([("V", vab)] + c.allv("G")) if dc == 0 else ())
        c.op("gpsimd", lambda g, of_=of: g.tensor_tensor(of_, of_, abc[:, W:2 * W], ALU.add))
        c.op("gpsimd", lambda g, of_=of, ob_=ob: g.tensor_copy(ob_, of_),
             incs=[("P", 1)] if dc == NDC - 1 else ())
    return c.val("P")


def proj_featmap(mk, c, wT, brow, xsb, dst_fn, extra_wait=None, extra_vwait=None):
    PS0 = mk.PS[0]
    r_ = mk.FM_R
    n_ = mk.FM_N
    for jb in range(NDC):
        w0 = [("V", c.val("V"))]
        if jb == 0 and extra_wait:
            w0 = w0 + extra_wait
        c.op("tensor", lambda t, jb_=jb: t.matmul(
            PS0[:, 0:SUP], brow[0:1, jb_ * 128:(jb_ + 1) * 128],
            mk.ONES1B, start=True, stop=False, skip_group_check=True), waits=w0)
        for dc in range(NDC):
            c.op("tensor", lambda t, jb_=jb, dc_=dc: t.matmul(
                PS0[:, 0:SUP], wT[:, dc_ * 1024 + jb_ * 128: dc_ * 1024 + (jb_ + 1) * 128],
                xsb[:, dc_ * SUP:(dc_ + 1) * SUP],
                start=False, stop=(dc_ == NDC - 1), skip_group_check=True),
                incs=[("T", 1)] if dc == NDC - 1 else ())
        tmm = c.val("T")
        c.op("vector", lambda v: v.tensor_scalar(r_, PS0[:, 0:SUP], 0.0, None, ALU.max),
             waits=[("T", tmm)] + (extra_vwait if (jb == 0 and extra_vwait) else []))
        c.op("vector", lambda v: v.tensor_scalar(n_, PS0[:, 0:SUP], 0.0, None, ALU.min),
             incs=[("V", 1)])
        vmn = c.val("V")
        c.op("scalar", lambda s: s.activation(n_, n_, AF.Exp), waits=[("V", vmn)],
             incs=[("S", 1)])
        sex = c.val("S")
        dst = dst_fn(jb)
        c.op("vector", lambda v, d=dst: v.tensor_tensor(d, r_, n_, ALU.add),
             waits=[("S", sex)], incs=[("V", 1)])


def ph_l0(mk: MK):
    c, ar = mk.ctx, mk.ar
    x0 = mk.stream("x0f", "in")
    x1f = mk.stream("x1f", "out" if mk.mode != "fused" else "scratch")
    x1b = mk.stream("x1b", "out" if mk.mode != "fused" else "scratch")
    vstash = mk.hbm("l0_vstash", (NCH, 128, NH * 65), BF16, "scratch")
    kstash = mk.hbm("l0_kstash", (NSUP, 128, NDC * SUP), BF16, "scratch")
    xin = mk.hbm("l0_xin", (128, 8 * 65), F32, "scratch")
    xout = mk.hbm("l0_xout", (256, 8 * 65), F32, "scratch")

    SST = ar.bbuf(NCH * 520)
    SINIT = ar.bbuf(520)
    XCH = ar.fbuf(520)
    FST = ar.fbuf(520)
    mk.FM_R = ar.fbuf(SUP)
    mk.FM_N = ar.fbuf(SUP)

    STP = [mk.PS[2], mk.PS[3], mk.PS[4], mk.PS[5]]
    PS0, PS1 = mk.PS[0], mk.PS[1]
    PB0, PB1 = mk.PB[0], mk.PB[1]

    m0 = ar.mark()
    WK = ar.bbuf(NDC * 1024)
    WV = ar.bbuf(NDC * 1024)
    BKV = ar.bbuf(2048, p=1)
    KSUP = ar.bbuf(NDC * SUP)
    XSB = ar.bbuf(NDC * SUP)
    VPAD = ar.bbuf(NCHS * NH * 65)
    KTM = ar.bbuf(1024)
    load_wT(mk, c, "wk", D, D, WK)
    load_wT(mk, c, "wv", D, D, WV)
    load_row(mk, c, "L0_attn_bk", D, BKV[0:1, 0:1024])
    load_row(mk, c, "L0_attn_bv", D, BKV[0:1, 1024:2048])
    wld, gld = c.val("L"), c.val("G")

    for sup in range(DBG["nsup_state"]):
        xcols = x0[:, sup * SUP:(sup + 1) * SUP].rearrange("(dc p) t -> p dc t", p=128)
        c.op("gpsimd", lambda g, i=xcols: g.dma_start(
            out=XSB.rearrange("p (dc t) -> p dc t", dc=NDC), in_=i),
            waits=c.allv("T"), incs=[("G", 16)])
        gx = c.val("G")
        ew = [("G", gx), ("L", wld)] + ([("G", gld)] if sup == 0 else [])
        proj_featmap(mk, c, WK, BKV[0:1, 0:1024], XSB,
                     lambda jb: KSUP[:, jb * SUP:(jb + 1) * SUP], extra_wait=ew,
                     extra_vwait=c.allv("G"))
        kdone = c.val("V")
        c.op("gpsimd", lambda g, sp=sup: g.dma_start(out=kstash[sp, :, :], in_=KSUP),
             waits=[("V", kdone)], incs=[("G", 16)])
        c.op("vector", lambda v: v.memset(VPAD, 1.0),
             waits=c.allv("G", "T"), incs=[("V", 1)])
        vst = c.val("V")
        for ci in range(NCHS):
            for half in range(2):
                c.op("tensor", lambda t, hf=half: t.matmul(
                    PS1[:, 0:SUP], mk.ONES1B[0:1, 0:128],
                    BKV[0:1, 1024 + hf * 512: 1024 + (hf + 1) * 512],
                    start=True, stop=False, skip_group_check=True),
                    waits=[("V", vst if (ci == 0 and half == 0) else c.val("V"))])
                for dc in range(NDC):
                    c.op("tensor", lambda t, ci_=ci, hf=half, dc_=dc: t.matmul(
                        PS1[:, 0:SUP],
                        XSB[:, dc_ * SUP + ci_ * CH: dc_ * SUP + (ci_ + 1) * CH],
                        WV[:, dc_ * 1024 + hf * 512: dc_ * 1024 + (hf + 1) * 512],
                        start=False, stop=(dc_ == NDC - 1), skip_group_check=True),
                        incs=[("T", 1)] if dc == NDC - 1 else ())
                tvm = c.val("T")
                dst = VPAD.rearrange("p (ci h e) -> p ci h e", ci=NCHS, h=NH)[
                    :, ci, 8 * half:8 * half + 8, 0:64]
                c.op("vector", lambda v, d_=dst: v.tensor_copy(
                    d_, PS1[:, 0:SUP].rearrange("p (h e) -> p h e", h=8)),
                    waits=[("T", tvm)], incs=[("V", 1)])
        vpdone = c.val("V")
        for ci in range(DBG["nchs"]):
            ch = sup * NCHS + ci
            for dc in range(NDC):
                c.op("tensor", lambda t, dc_=dc, ci_=ci: t.transpose(
                    PB0[:, dc_ * 128:(dc_ + 1) * 128],
                    KSUP[:, dc_ * SUP + ci_ * CH: dc_ * SUP + (ci_ + 1) * CH], mk.ID_B),
                    waits=(c.allv("V") if dc == 0 else ()),
                    incs=[("T", 1)] if dc == NDC - 1 else ())
            tkt = c.val("T")
            c.op("vector", lambda v: v.tensor_copy(KTM, PB0[:, 0:1024]),
                 waits=[("T", tkt)], incs=[("V", 1)])
            vktm = c.val("V")
            if ch == 0:
                c.op("vector", lambda v: v.memset(SST[:, 0:520], 0.0), incs=[("V", 1)])
            else:
                for q in range(4):
                    c.op("vector", lambda v, q_=q, ch_=ch: v.tensor_copy(
                        SST[:, ch_ * 520 + q_ * 130: ch_ * 520 + (q_ + 1) * 130],
                        STP[q_][:, 0:130]),
                        waits=(c.allv("T") if q == 0 else ()),
                        incs=[("V", 1)] if q == 3 else ())
            vsst = c.val("V")
            for h in range(NH):
                q, j = (h // 2) // 2, (h // 2) % 2
                ro = 64 * (h % 2)
                tp = (0, 64) if (h % 2) else None
                vslice = VPAD.rearrange("p (ci h e) -> p ci h e", ci=NCHS, h=NH)[:, ci, h, :]
                c.op("tensor", lambda t, h_=h, q_=q, j_=j, ro_=ro, tp_=tp, vs=vslice, ch_=ch:
                     t.matmul(STP[q_][ro_:ro_ + 64, j_ * 65:(j_ + 1) * 65],
                              KTM[:, h_ * 64:(h_ + 1) * 64], vs,
                              start=(ch_ == 0 and (h_ % 4) in (0, 1)),
                              stop=True, skip_group_check=True, tile_position=tp_),
                     waits=([("V", vktm), ("V", vsst), ("V", vpdone)] if h == 0 else ()),
                     incs=[("T", 1)] if h == NH - 1 else ())
        for ci in range(NCHS):
            ch = sup * NCHS + ci
            c.op("gpsimd", lambda g, ch_=ch, ci_=ci: g.dma_start(
                out=vstash[ch_, :, :], in_=VPAD[:, ci_ * 1040:(ci_ + 1) * 1040]),
                waits=[("V", vpdone)], incs=[("G", 16)])
    if DBG["skip_exchange"]:
        c.op("vector", lambda v: v.memset(SINIT[:, 0:520], 0.0), incs=[("V", 1)])
    tfin = c.val("T")
    if not DBG["skip_exchange"]:
        for q in range(4):
            c.op("vector", lambda v, q_=q: v.tensor_copy(
                FST[:, q_ * 130:(q_ + 1) * 130], STP[q_][:, 0:130]),
                waits=([("T", tfin)] if q == 0 else ()), incs=[("V", 1)] if q == 3 else ())
        vfin = c.val("V")
        c.op("gpsimd", lambda g: g.dma_start(out=xin, in_=FST[:, 0:520]),
             waits=[("V", vfin)], incs=[("G", 16)])
        gxo = c.val("G")
        c.op("gpsimd", lambda g: g.collective_compute(
            "AllGather", ALU.bypass, replica_groups=RG_PAIRS,
            ins=[xin.opt()], outs=[xout.opt()]),
            waits=[("G", gxo)], incs=[("CC", 1)])
        ccv = c.val("CC")
        c.op("gpsimd", lambda g: g.dma_start(out=XCH[:, 0:520], in_=xout[0:128, :]),
             waits=[("CC", ccv)], incs=[("G", 16)])
        gxi = c.val("G")
        c.op("vector", lambda v: v.tensor_scalar(SINIT[:, 0:520], XCH[:, 0:520],
                                                 mk.GATE[:, 0:1], None, ALU.mult),
             waits=[("G", gxi)], incs=[("V", 1)])
    c.barrier()
    ar.release(m0)
    if DBG["skip_ytail"] == 2:
        WQx = ar.bbuf(NDC * 1024)
        load_wT(mk, c, "wq", D, D, WQx)
        c.barrier()
        return
    if DBG["skip_ytail"] == 3:
        BQx = ar.bbuf(1024, p=1)
        load_row(mk, c, "L0_attn_bq", D, BQx[0:1, 0:1024])
        c.barrier()
        return
    if DBG["skip_ytail"] == 4:
        c.barrier()
        return
    if DBG["skip_ytail"] == 5:
        WQx = ar.bbuf(NDC * 1024)
        WOx = ar.bbuf(NDC * 1024)
        BQx = ar.bbuf(2048, p=1)
        load_wT(mk, c, "wq", D, D, WQx)
        load_wT(mk, c, "wo", D, D, WOx)
        load_row(mk, c, "L0_attn_bq", D, BQx[0:1, 0:1024])
        load_row(mk, c, "L0_attn_bo", D, BQx[0:1, 1024:2048])
        c.barrier()
        return
    if DBG["skip_ytail"] == 6:
        WQx = ar.bbuf(NDC * 1024)
        WOx = ar.bbuf(NDC * 1024)
        BQx = ar.bbuf(2048, p=1)
        _qf = ar.bbuf(NDC * SUP); _ks = ar.bbuf(NDC * SUP); _xs = ar.bbuf(NDC * SUP)
        _xf = ar.fbuf(NDC * SUP); _st = ar.bbuf(520); _vl = ar.bbuf(NH * 65)
        _sm = ar.bbuf(128); _at = ar.bbuf(D); _af = ar.bbuf(NDC * SUP)
        _rc = ar.fbuf(16); _zs = ar.fbuf(NDC * SUP); _lr = ar.fbuf(6 * SUP, p=1)
        _la = ar.fbuf(2 * SUP); _lf = ar.fbuf(NDC * SUP); _lb = ar.bbuf(NDC * SUP)
        load_wT(mk, c, "wq", D, D, WQx)
        load_wT(mk, c, "wo", D, D, WOx)
        load_row(mk, c, "L0_attn_bq", D, BQx[0:1, 0:1024])
        load_row(mk, c, "L0_attn_bo", D, BQx[0:1, 1024:2048])
        c.barrier()
        return
    if DBG["skip_ytail"]:
        return

    WQ = ar.bbuf(NDC * 1024)
    WO = ar.bbuf(NDC * 1024)
    BQO = ar.bbuf(2048, p=1)
    QF = ar.bbuf(NDC * SUP)
    KSY = ar.bbuf(NDC * SUP)
    XSY = ar.bbuf(NDC * SUP)
    XSF = ar.fbuf(NDC * SUP)
    STRU = ar.bbuf(520)
    VLD = ar.bbuf(NH * 65)
    SM = ar.bbuf(128)
    ATM = ar.bbuf(D)
    AFM = ar.bbuf(NDC * SUP)
    RCOL = ar.fbuf(16)
    mk.DBGSCR = ar.fbuf(80)
    ZSB = ar.fbuf(NDC * SUP)
    mk.LN_ZB = AFM
    mk.LN_ROWS = ar.fbuf(6 * SUP, p=1)
    mk.LN_ABC = ar.fbuf(2 * SUP)
    LNB = ar.bbuf(NDC * SUP)
    mk.LN_SQ = LNB

    load_wT(mk, c, "wq", D, D, WQ)
    load_wT(mk, c, "wo", D, D, WO)
    load_row(mk, c, "L0_attn_bq", D, BQO[0:1, 0:1024])
    load_row(mk, c, "L0_attn_bo", D, BQO[0:1, 1024:2048])
    wld2, gld2 = c.val("L"), c.val("G")
    vsin = c.val("V")

    for sup in range(DBG["nsup_y"]):
        xcols = x0[:, sup * SUP:(sup + 1) * SUP].rearrange("(dc p) t -> p dc t", p=128)
        c.op("gpsimd", lambda g, i=xcols: g.dma_start(
            out=XSY.rearrange("p (dc t) -> p dc t", dc=NDC), in_=i),
            waits=c.allv("T"), incs=[("G", 16)])
        c.op("sync", lambda s, i=xcols: s.dma_start(
            out=XSF.rearrange("p (dc t) -> p dc t", dc=NDC), in_=i),
            waits=c.allv("P", "V"), incs=[("L", 16)])
        c.op("sync", lambda s, sp=sup: s.dma_start(out=KSY, in_=kstash[sp, :, :]),
             waits=c.allv("T"), incs=[("L", 16)])
        gx, lx = c.val("G"), c.val("L")
        ew = [("G", gx), ("L", wld2 if sup > 0 else lx)] + ([("G", gld2)] if sup == 0 else [])
        proj_featmap(mk, c, WQ, BQO[0:1, 0:1024], XSY,
                     lambda jb: QF[:, jb * SUP:(jb + 1) * SUP], extra_wait=ew,
                     extra_vwait=c.allv("G"))
        qdone = c.val("V")
        for ci in range(NCHS):
            ch = sup * NCHS + ci
            c.op("sync", lambda s, ch_=ch: s.dma_start(out=VLD, in_=vstash[ch_, :, :]),
                 waits=c.allv("T"), incs=[("L", 16)])
            lvl = c.val("L")
            c.op("vector", lambda v, ch_=ch: v.tensor_tensor(
                STRU[:, 0:520], SST[:, ch_ * 520:(ch_ + 1) * 520], SINIT[:, 0:520], ALU.add),
                waits=c.allv("T") + (c.allv("G") if "dbg_atm" in mk.taps else [])
                + ([("V", vsin)] if (sup == 0 and ci == 0) else []),
                incs=[("V", 1)])
            vstru = c.val("V")
            for h in range(NH):
                ro = 64 * (h % 2)
                jb = h // 2
                c.op("tensor", lambda t, jb_=jb, ro_=ro, ci_=ci: t.matmul(
                    PS1[:, 0:128],
                    KSY[ro_:ro_ + 64, jb_ * SUP + ci_ * CH: jb_ * SUP + (ci_ + 1) * CH],
                    QF[ro_:ro_ + 64, jb_ * SUP + ci_ * CH: jb_ * SUP + (ci_ + 1) * CH],
                    start=True, stop=True, skip_group_check=True),
                    waits=[("V", qdone if h == 0 else c.val("V")), ("L", lx)],
                    incs=[("T", 1)])
                tst_ = c.val("T")
                c.op("vector", lambda v: v.tensor_tensor(SM, PS1[:, 0:128], mk.MASK, ALU.mult),
                     waits=[("T", tst_)], incs=[("V", 1)])
                vsm = c.val("V")
                c.op("tensor", lambda t, h_=h: t.matmul(
                    PS0[:, 0:65], SM, VLD[:, h_ * 65:(h_ + 1) * 65],
                    start=True, stop=False, skip_group_check=True),
                    waits=[("V", vsm), ("L", lvl)])
                c.op("tensor", lambda t, jb_=jb, ro_=ro, ci_=ci: t.matmul(
                    PS0[:, 0:65],
                    QF[ro_:ro_ + 64, jb_ * SUP + ci_ * CH: jb_ * SUP + (ci_ + 1) * CH],
                    STRU[ro_:ro_ + 64, jb_ * 65:(jb_ + 1) * 65],
                    start=False, stop=True, skip_group_check=True),
                    waits=[("V", vstru)], incs=[("T", 1)])
                tnum = c.val("T")
                if "dbg_num" in mk.taps and ch == 0 and h in (0, 2):
                    dnum = mk.hbm("dbg_num", (2, 128, 65), F32, "out")
                    dsm = mk.hbm("dbg_sm", (2, 128, 128), BF16, "out")
                    slot = 0 if h == 0 else 1
                    DSC = mk.DBGSCR
                    c.op("vector", lambda v, sl=slot: v.tensor_copy(DSC[:, 0:65], PS0[:, 0:65]),
                         waits=[("T", tnum)], incs=[("V", 1)])
                    dv = c.val("V")
                    c.op("gpsimd", lambda g, sl=slot: g.dma_start(out=dnum[sl, :, :], in_=DSC[:, 0:65]),
                         waits=[("V", dv)], incs=[("G", 16)])
                    c.op("gpsimd", lambda g, sl=slot: g.dma_start(out=dsm[sl, :, :], in_=SM),
                         waits=[("V", dv)], incs=[("G", 16)])
                    gdn = c.val("G")
                    c.op("vector", lambda v: v.tensor_scalar(RCOL[:, 2:3], RCOL[:, 2:3], 0.0, None, ALU.max),
                         waits=[("G", gdn)], incs=[("V", 1)])
                c.op("vector", lambda v: v.tensor_scalar(RCOL[:, 0:1], PS0[:, 64:65],
                                                         1e-6, None, ALU.max),
                     waits=[("T", tnum)], incs=[("V", 1)])
                va = c.val("V")
                c.op("vector", lambda v: v.reciprocal(RCOL[:, 1:2], RCOL[:, 0:1]),
                     waits=[("V", va)], incs=[("V", 1)])
                vb = c.val("V")
                c.op("vector", lambda v, h_=h: v.tensor_scalar(
                    ATM[:, h_ * 64:(h_ + 1) * 64], PS0[:, 0:64], RCOL[:, 1:2], None, ALU.mult),
                    waits=[("V", vb)], incs=[("V", 1)])
            vatm = c.val("V")
            if "dbg_atm" in mk.taps:
                datm = mk.hbm("dbg_atm", (NCH, 128, D), BF16, "out")
                c.op("gpsimd", lambda g, ch_=ch: g.dma_start(out=datm[ch_, :, :], in_=ATM),
                     waits=[("V", vatm)], incs=[("G", 16)])
            for dc in range(NDC):
                c.op("tensor", lambda t, dc_=dc: t.transpose(
                    PB1[:, dc_ * 128:(dc_ + 1) * 128],
                    ATM[:, dc_ * 128:(dc_ + 1) * 128], mk.ID_B),
                    waits=([("V", vatm)] if dc == 0 else ()),
                    incs=[("T", 1)] if dc == NDC - 1 else ())
            tat = c.val("T")
            c.op("vector", lambda v, ci_=ci: v.tensor_copy(
                AFM.rearrange("p (dc t) -> p dc t", dc=NDC)[:, :, ci_ * CH:(ci_ + 1) * CH],
                PB1[:, 0:1024].rearrange("p (dc t) -> p dc t", dc=NDC)),
                waits=[("T", tat)], incs=[("V", 1)])
        vafm = c.val("V")
        for jb in range(NDC):
            c.op("tensor", lambda t, jb_=jb: t.matmul(
                PS1[:, 0:SUP], BQO[0:1, 1024 + jb_ * 128: 1024 + (jb_ + 1) * 128],
                mk.ONES1B, start=True, stop=False, skip_group_check=True),
                waits=[("V", vafm if jb == 0 else c.val("V"))] + (c.allv("P") if jb == 0 else []))
            for dc in range(NDC):
                c.op("tensor", lambda t, jb_=jb, dc_=dc: t.matmul(
                    PS1[:, 0:SUP], WO[:, dc_ * 1024 + jb_ * 128: dc_ * 1024 + (jb_ + 1) * 128],
                    AFM[:, dc_ * SUP:(dc_ + 1) * SUP],
                    start=False, stop=(dc_ == NDC - 1), skip_group_check=True),
                    incs=[("T", 1)] if dc == NDC - 1 else ())
            two = c.val("T")
            c.op("vector", lambda v, jb_=jb: v.tensor_tensor(
                ZSB[:, jb_ * SUP:(jb_ + 1) * SUP], PS1[:, 0:SUP],
                XSF[:, jb_ * SUP:(jb_ + 1) * SUP], ALU.add),
                waits=[("T", two), ("L", lx)] + (c.allv("P") if jb == 0 else []),
                incs=[("V", 1)])
        pdone = ln_fm(mk, c, lambda dc: ZSB[:, dc * SUP:(dc + 1) * SUP], SUP,
                      lambda dc: LNB[:, dc * SUP:(dc + 1) * SUP],
                      mk.PS[2], mk.PS[3], mk.PS[4])
        for dst, src_ in ((x1f, ZSB), (x1b, LNB)):
            c.op("gpsimd", lambda g, d_=dst, s_=src_, sp=sup: g.dma_start(
                out=d_[:, sp * SUP:(sp + 1) * SUP].rearrange("(dc p) t -> p dc t", p=128),
                in_=s_.rearrange("p (dc t) -> p dc t", dc=NDC)),
                waits=[("P", pdone)], incs=[("G", 16)])
    c.barrier()


def ph_router(mk: MK):
    c, ar = mk.ctx, mk.ar
    x1f = mk.stream("x1f", "in")
    comb_h = mk.hbm("moe_comb", (N_EXP, T), BF16, "scratch")
    m0 = ar.mark()
    RWS = ar.fbuf(1024, p=32)
    RWT = ar.fbuf(NDC * 32)
    EBT = ar.fbuf(32)
    EBROW = ar.fbuf(32, p=1)
    XSPAN = ar.fbuf(NDC * SUP)
    PRB = ar.fbuf(32)
    CRR = ar.fbuf(32)
    SEL = ar.fbuf(32)
    GE = ar.fbuf(32)
    TW = ar.fbuf(32)
    CMBF = ar.fbuf(32)
    CMBB = ar.bbuf(32)
    CMBT = ar.bbuf(T, p=32)
    RC = ar.fbuf(16)
    tbc = mk.inline("c_tb", (-1e-6 * np.broadcast_to(np.arange(32, dtype=np.float32), (128, 32))).copy())
    rw_in = mk.hbm("i_L0_ffn_router_weight", (N_EXP, D), F32, "in")
    eb_in = mk.hbm("i_L0_ffn_e_score_bias", (N_EXP,), F32, "in")
    c.op("sync", lambda s_: s_.dma_start(out=RWS[0:32, :], in_=rw_in), incs=[("L", 16)])
    c.op("sync", lambda s_: s_.dma_start(out=EBROW[0:1, :], in_=eb_in.unsqueeze(0)), incs=[("L", 16)])
    c.op("sync", lambda s_: s_.dma_start(out=EBT, in_=tbc), incs=[("L", 16)])
    lrw = c.val("L")
    PS0, PS1 = mk.PS[0], mk.PS[1]
    for dc in range(NDC):
        c.op("tensor", lambda t, dc_=dc: t.transpose(
            PS1[:, dc_ * 32:(dc_ + 1) * 32], RWS[0:32, dc_ * 128:(dc_ + 1) * 128], mk.ID_F[0:32, 0:32]),
            waits=([("L", lrw)] if dc == 0 else ()),
            incs=([("T", 1)] if dc == NDC - 1 else ()))
    trw = c.val("T")
    c.op("vector", lambda v: v.tensor_copy(RWT, PS1[:, 0:256]), waits=[("T", trw)], incs=[("V", 1)])
    vrw = c.val("V")
    c.op("tensor", lambda t: t.matmul(PS0[:, 0:32], mk.ONES1F[0:1, 0:128], EBROW[0:1, 0:32],
                                      start=True, stop=True, skip_group_check=True),
         waits=[("L", lrw), ("V", vrw)], incs=[("T", 1)])
    teb = c.val("T")
    c.op("vector", lambda v: v.tensor_add(EBT, EBT, PS0[:, 0:32]), waits=[("T", teb)], incs=[("V", 1)])
    veb = c.val("V")
    for sup in range(NSUP):
        xc = x1f[:, sup * SUP:(sup + 1) * SUP].rearrange("(dc p) t -> p dc t", p=128)
        c.op("sync", lambda s_, i=xc: s_.dma_start(
            out=XSPAN.rearrange("p (dc t) -> p dc t", dc=NDC), in_=i),
            waits=c.allv("T"), incs=[("L", 16)])
        lx = c.val("L")
        for ci in range(NCHS):
            ch = sup * NCHS + ci
            for dc in range(NDC):
                c.op("tensor", lambda t, dc_=dc, ci_=ci: t.matmul(
                    PS0[:, 0:32], XSPAN[:, dc_ * SUP + ci_ * CH: dc_ * SUP + (ci_ + 1) * CH],
                    RWT[:, dc_ * 32:(dc_ + 1) * 32],
                    start=(dc_ == 0), stop=(dc_ == NDC - 1), skip_group_check=True),
                    waits=([("L", lx), ("V", c.val("V"))] if dc == 0 else ()),
                    incs=([("T", 1)] if dc == NDC - 1 else ()))
            tlg = c.val("T")
            c.op("vector", lambda v: v.reduce_max(RC[:, 0:1], PS0[:, 0:32], axis=AX.X, negate=True),
                 waits=[("T", tlg)], incs=[("V", 1)])
            vm = c.val("V")
            c.op("scalar", lambda s_: s_.activation(PRB, PS0[:, 0:32], AF.Exp, bias=RC[:, 0:1]),
                 waits=[("V", vm)], incs=[("S", 1)])
            se = c.val("S")
            c.op("vector", lambda v: v.reduce_sum(RC[:, 1:2], PRB, axis=AX.X),
                 waits=[("S", se)], incs=[("V", 1)])
            vs_ = c.val("V")
            c.op("vector", lambda v: v.reciprocal(RC[:, 2:3], RC[:, 1:2]),
                 waits=[("V", vs_)], incs=[("V", 1)])
            vr = c.val("V")
            c.op("vector", lambda v: v.tensor_scalar(PRB, PRB, RC[:, 2:3], None, ALU.mult),
                 waits=[("V", vr)], incs=[("V", 1)])
            c.op("vector", lambda v: v.tensor_add(CRR, PRB, EBT),
                 waits=[("V", veb)], incs=[("V", 1)])
            c.op("vector", lambda v: v.memset(SEL, 0.0), incs=[("V", 1)])
            for k6 in range(TOPK):
                c.op("vector", lambda v: v.reduce_max(RC[:, 3:4], CRR, axis=AX.X),
                     incs=[("V", 1)])
                vmx = c.val("V")
                c.op("vector", lambda v: v.tensor_scalar(GE, CRR, RC[:, 3:4], None, ALU.is_ge),
                     waits=[("V", vmx)], incs=[("V", 1)])
                c.op("vector", lambda v: v.tensor_add(SEL, SEL, GE), incs=[("V", 1)])
                if k6 < TOPK - 1:
                    c.op("vector", lambda v: v.scalar_tensor_tensor(CRR, GE, -1e9, CRR, ALU.mult, ALU.add),
                         incs=[("V", 1)])
            c.op("vector", lambda v: v.tensor_mul(TW, PRB, SEL), incs=[("V", 1)])
            c.op("vector", lambda v: v.reduce_sum(RC[:, 4:5], TW, axis=AX.X), incs=[("V", 1)])
            vs6 = c.val("V")
            c.op("vector", lambda v: v.tensor_scalar(RC[:, 5:6], RC[:, 4:5], 1e-12, None, ALU.max),
                 waits=[("V", vs6)], incs=[("V", 1)])
            vmx6 = c.val("V")
            c.op("vector", lambda v: v.reciprocal(RC[:, 6:7], RC[:, 5:6]),
                 waits=[("V", vmx6)], incs=[("V", 1)])
            vr6 = c.val("V")
            c.op("vector", lambda v: v.tensor_scalar(CMBF, TW, RC[:, 6:7], None, ALU.mult),
                 waits=[("V", vr6)], incs=[("V", 1)])
            c.op("vector", lambda v: v.tensor_copy(CMBB, CMBF), incs=[("V", 1)])
            vcm = c.val("V")
            c.op("tensor", lambda t: t.transpose(mk.PB[0][0:32, 0:128], CMBB, mk.ID_B),
                 waits=[("V", vcm)], incs=[("T", 1)])
            tcm = c.val("T")
            c.op("vector", lambda v, ch_=ch: v.tensor_copy(
                CMBT[0:32, ch_ * CH:(ch_ + 1) * CH], mk.PB[0][0:32, 0:128]),
                waits=[("T", tcm)], incs=[("V", 1)])
    vdone = c.val("V")
    c.op("gpsimd", lambda g: g.dma_start(out=comb_h, in_=CMBT[0:32, :]),
         waits=[("V", vdone)], incs=[("G", 16)])
    c.barrier()
    ar.release(m0)


def ph_moe(mk: MK):
    c, ar = mk.ctx, mk.ar
    x1f = mk.stream("x1f", "in")
    x1b = mk.stream("x1b", "in")
    x2f = mk.stream("x2f", "out" if mk.mode != "fused" else "scratch")
    x2b = mk.stream("x2b", "out" if mk.mode != "fused" else "scratch")
    comb_h = mk.hbm("moe_comb", (N_EXP, T), BF16, "scratch")
    HT = T // 4
    NSP = HT // SUP
    PS0, PS1, PS2, PS3 = mk.PS[0], mk.PS[1], mk.PS[2], mk.PS[3]

    for half in range(4):
        mh = ar.mark()
        XBH = ar.bbuf(NDC * HT)
        YA = ar.fbuf(NDC * HT)
        SG = ar.bbuf(8 * SUP)
        HM = ar.bbuf(8 * SUP)
        CBB = ar.bbuf(SUP)
        CROW = ar.bbuf(2 * HT, p=1)
        xcols = x1b[:, half * HT:(half + 1) * HT].rearrange("(dc p) t -> p dc t", p=128)
        c.op("sync", lambda s_, i=xcols: s_.dma_start(
            out=XBH.rearrange("p (dc t) -> p dc t", dc=NDC), in_=i),
            waits=c.allv("T", "P"), incs=[("L", 16)])
        lxb = c.val("L")

        ms = ar.mark()
        SW = ar.bbuf(3 * NDC * 1024)
        load_wT(mk, c, "sgw", D, IS, SW[:, 0:8192])
        load_wT(mk, c, "suw", D, IS, SW[:, 8192:16384])
        load_wT(mk, c, "sdw", IS, D, SW[:, 16384:24576])
        lsw = c.val("L")
        for sp in range(NSP):
            t0 = sp * SUP
            for phase_i, dst in ((0, None), (1, None)):
                for ib in range(NDC):
                    ps = PS0 if (ib % 2 == 0) else PS1
                    for dc in range(NDC):
                        c.op("tensor", lambda t, ib_=ib, dc_=dc, ph_=phase_i, ps_=ps, t0_=t0: t.matmul(
                            ps_[:, 0:SUP],
                            SW[:, ph_ * 8192 + dc_ * 1024 + ib_ * 128: ph_ * 8192 + dc_ * 1024 + (ib_ + 1) * 128],
                            XBH[:, dc_ * HT + t0_: dc_ * HT + t0_ + SUP],
                            start=(dc_ == 0), stop=(dc_ == NDC - 1), skip_group_check=True),
                            waits=([("L", lsw), ("L", lxb), ("V", c.val("V")), ("S", c.val("S"))] if dc == 0 else ()),
                            incs=([("T", 1)] if dc == NDC - 1 else ()))
                    tg = c.val("T")
                    if phase_i == 0:
                        c.op("scalar", lambda s_, ib_=ib, ps_=ps: s_.activation(
                            SG[:, ib_ * SUP:(ib_ + 1) * SUP], ps_[:, 0:SUP], AF.Silu),
                            waits=[("T", tg)], incs=[("S", 1)])
                    else:
                        c.op("vector", lambda v, ib_=ib, ps_=ps: v.tensor_mul(
                            HM[:, ib_ * SUP:(ib_ + 1) * SUP], SG[:, ib_ * SUP:(ib_ + 1) * SUP],
                            ps_[:, 0:SUP]),
                            waits=[("T", tg), ("S", c.val("S"))], incs=[("V", 1)])
            vhm = c.val("V")
            for jb in range(NDC):
                ps = PS2 if (jb % 2 == 0) else PS3
                for ic in range(NDC):
                    c.op("tensor", lambda t, jb_=jb, ic_=ic, ps_=ps: t.matmul(
                        ps_[:, 0:SUP],
                        SW[:, 16384 + ic_ * 1024 + jb_ * 128: 16384 + ic_ * 1024 + (jb_ + 1) * 128],
                        HM[:, ic_ * SUP:(ic_ + 1) * SUP],
                        start=(ic_ == 0), stop=(ic_ == NDC - 1), skip_group_check=True),
                        waits=([("V", vhm)] if (jb == 0 and ic == 0) else
                               ([("V", c.val("V"))] if ic == 0 else ())),
                        incs=([("T", 1)] if ic == NDC - 1 else ()))
                td = c.val("T")
                c.op("vector", lambda v, jb_=jb, ps_=ps, t0_=t0: v.tensor_copy(
                    YA[:, jb_ * HT + t0_: jb_ * HT + t0_ + SUP], ps_[:, 0:SUP]),
                    waits=[("T", td)], incs=[("V", 1)])
        c.barrier()
        ar.release(ms)

        WBUF = ar.bbuf(2 * 12 * 1024)
        wsem_free = [0, 0]
        gup_scr = mk.ht["scr_gup"]
        dwn_scr = mk.ht["scr_dwn"]
        for e in range(N_EXP):
            slot = e % 2
            wbase = slot * 12 * 1024
            wv = [("T", wsem_free[slot])] if wsem_free[slot] > 0 else []
            for dc in range(NDC):
                c.op("sync", lambda s_, dc_=dc, e_=e, wb=wbase: s_.dma_start(
                    out=WBUF[:, wb + dc_ * 1024: wb + (dc_ + 1) * 1024],
                    in_=gup_scr[e_ * 1024:(e_ + 1) * 1024, dc_ * 128:(dc_ + 1) * 128],
                    transpose=True),
                    waits=(wv if dc == 0 else ()), incs=[("L", 16)])
            for ic in range(4):
                c.op("sync", lambda s_, ic_=ic, e_=e, wb=wbase: s_.dma_start(
                    out=WBUF[:, wb + 8192 + ic_ * 1024: wb + 8192 + (ic_ + 1) * 1024],
                    in_=dwn_scr[e_ * 1024:(e_ + 1) * 1024, ic_ * 128:(ic_ + 1) * 128],
                    transpose=True),
                    incs=[("L", 16)])
            lw = c.val("L")
            crow_sl = CROW[0:1, slot * HT:(slot + 1) * HT]
            crow_src = comb_h[e:e + 1, half * HT:(half + 1) * HT]
            c.op("sync", lambda s_, cs=crow_sl, ci_=crow_src: s_.dma_start(
                out=cs, in_=ci_),
                incs=[("L", 16)])
            lcr = c.val("L")
            for sp in range(NSP):
                t0 = sp * SUP
                c.op("tensor", lambda t, t0_=t0, cs=crow_sl: t.matmul(
                    mk.PS[4][:, 0:512], mk.ONES1B[0:1, 0:128], cs[0:1, t0_:t0_ + SUP],
                    start=True, stop=True, skip_group_check=True),
                    waits=[("L", lcr), ("V", c.val("V"))], incs=[("T", 1)])
                tcb = c.val("T")
                c.op("vector", lambda v: v.tensor_copy(CBB, mk.PS[4][:, 0:512]),
                     waits=[("T", tcb)], incs=[("V", 1)])
                vcb = c.val("V")
                for ib in range(NDC):
                    ps = PS0 if (ib % 2 == 0) else PS1
                    for dc in range(NDC):
                        c.op("tensor", lambda t, ib_=ib, dc_=dc, ps_=ps, t0_=t0, wb=wbase: t.matmul(
                            ps_[:, 0:SUP],
                            WBUF[:, wb + dc_ * 1024 + ib_ * 128: wb + dc_ * 1024 + (ib_ + 1) * 128],
                            XBH[:, dc_ * HT + t0_: dc_ * HT + t0_ + SUP],
                            start=(dc_ == 0), stop=(dc_ == NDC - 1), skip_group_check=True),
                            waits=([("L", lw), ("V", c.val("V")), ("S", c.val("S"))] if dc == 0 else ()),
                            incs=([("T", 1)] if dc == NDC - 1 else ()))
                    tg = c.val("T")
                    if ib < 4:
                        c.op("scalar", lambda s_, ib_=ib, ps_=ps: s_.activation(
                            SG[:, ib_ * SUP:(ib_ + 1) * SUP], ps_[:, 0:SUP], AF.Silu),
                            waits=[("T", tg)], incs=[("S", 1)])
                    else:
                        ic = ib - 4
                        c.op("vector", lambda v, ic_=ic, ps_=ps: v.tensor_mul(
                            HM[:, ic_ * SUP:(ic_ + 1) * SUP], SG[:, ic_ * SUP:(ic_ + 1) * SUP],
                            ps_[:, 0:SUP]),
                            waits=[("T", tg), ("S", c.val("S")), ("V", vcb)], incs=[("V", 1)])
                        c.op("vector", lambda v, ic_=ic: v.tensor_mul(
                            HM[:, ic_ * SUP:(ic_ + 1) * SUP], HM[:, ic_ * SUP:(ic_ + 1) * SUP],
                            CBB),
                            incs=[("V", 1)])
                vhm = c.val("V")
                for jb in range(NDC):
                    ps = PS2 if (jb % 2 == 0) else PS3
                    for ic in range(4):
                        c.op("tensor", lambda t, jb_=jb, ic_=ic, ps_=ps, wb=wbase: t.matmul(
                            ps_[:, 0:SUP],
                            WBUF[:, wb + 8192 + ic_ * 1024 + jb_ * 128: wb + 8192 + ic_ * 1024 + (jb_ + 1) * 128],
                            HM[:, ic_ * SUP:(ic_ + 1) * SUP],
                            start=(ic_ == 0), stop=(ic_ == 3), skip_group_check=True),
                            waits=([("V", vhm)] if (jb == 0 and ic == 0) else
                                   ([("V", c.val("V"))] if ic == 0 else ())),
                            incs=([("T", 1)] if ic == 3 else ()))
                    td = c.val("T")
                    c.op("vector", lambda v, jb_=jb, ps_=ps, t0_=t0: v.tensor_add(
                        YA[:, jb_ * HT + t0_: jb_ * HT + t0_ + SUP],
                        YA[:, jb_ * HT + t0_: jb_ * HT + t0_ + SUP], ps_[:, 0:SUP]),
                        waits=[("T", td)], incs=[("V", 1)])
            wsem_free[slot] = c.val("T")

        if "dbg_ya" in mk.taps:
            dya = mk.hbm("dbg_ya", (4, 128, NDC * HT), F32, "out")
            vw = c.val("V")
            c.op("gpsimd", lambda g, h_=half: g.dma_start(out=dya[h_, :, :], in_=YA),
                 waits=[("V", vw), ("T", c.val("T"))], incs=[("G", 16)])
        XF2 = ar.fbuf(NDC * SUP)
        mk.LN_ROWS = ar.fbuf(6 * SUP, p=1)
        mk.LN_ABC = ar.fbuf(2 * SUP)
        LNB2 = ar.bbuf(NDC * SUP)
        LZB2 = ar.bbuf(NDC * SUP)
        mk.LN_ZB = LZB2
        mk.LN_SQ = LNB2
        for sp in range(NSP):
            gt = half * HT + sp * SUP
            xc2 = x1f[:, gt:gt + SUP].rearrange("(dc p) t -> p dc t", p=128)
            c.op("sync", lambda s_, i=xc2: s_.dma_start(
                out=XF2.rearrange("p (dc t) -> p dc t", dc=NDC), in_=i),
                waits=c.allv("V", "P"), incs=[("L", 16)])
            lx2 = c.val("L")
            for dc in range(NDC):
                c.op("vector", lambda v, dc_=dc, sp_=sp: v.tensor_add(
                    YA[:, dc_ * HT + sp_ * SUP: dc_ * HT + (sp_ + 1) * SUP],
                    YA[:, dc_ * HT + sp_ * SUP: dc_ * HT + (sp_ + 1) * SUP],
                    XF2[:, dc_ * SUP:(dc_ + 1) * SUP]),
                    waits=([("L", lx2)] + c.allv("P", "T") if dc == 0 else ()),
                    incs=([("V", 1)] if dc == NDC - 1 else ()))
            pd = ln_fm(mk, c, lambda dc, sp_=sp: YA[:, dc * HT + sp_ * SUP: dc * HT + (sp_ + 1) * SUP],
                       SUP,
                       lambda dc: LNB2[:, dc * SUP:(dc + 1) * SUP],
                       mk.PS[2], mk.PS[3], mk.PS[4])
            c.op("gpsimd", lambda g, gt_=gt, sp_=sp: g.dma_start(
                out=x2f[:, gt_:gt_ + SUP].rearrange("(dc p) t -> p dc t", p=128),
                in_=YA.rearrange("p (dc t) -> p dc t", dc=NDC)[:, :, sp_ * SUP:(sp_ + 1) * SUP]),
                waits=[("P", pd)], incs=[("G", 16)])
            c.op("gpsimd", lambda g, gt_=gt: g.dma_start(
                out=x2b[:, gt_:gt_ + SUP].rearrange("(dc p) t -> p dc t", p=128),
                in_=LNB2.rearrange("p (dc t) -> p dc t", dc=NDC)),
                incs=[("G", 16)])
        c.barrier()
        ar.release(mh)


def ph_prep_ffnT(mk: MK):
    c, ar = mk.ctx, mk.ar
    scrT = {}
    for nm in ("wg1", "wu1"):
        scrT[nm] = mk.hbm("scrT_" + nm, (D, FH), BF16, "scratch")
    m0 = ar.mark()
    TMP = ar.bbuf(FH)
    for nm in ("wg1", "wu1"):
        scr = mk.ht["scr_" + nm]
        for dc in range(NDC):
            c.op("sync", lambda s_, dc_=dc, sc=scr: s_.dma_start(
                out=TMP, in_=sc[0:FH, dc_ * 128:(dc_ + 1) * 128], transpose=True),
                waits=c.allv("G"), incs=[("L", 16)])
            lt = c.val("L")
            c.op("gpsimd", lambda g, dc_=dc, st=scrT[nm]: g.dma_start(
                out=st[dc_ * 128:(dc_ + 1) * 128, :], in_=TMP),
                waits=[("L", lt)], incs=[("G", 16)])
            gv = c.val("G")
            c.op("sync", lambda s_, gv_=gv: s_.wait_ge(c.sems["G"], gv_) or s_.nop())
    c.barrier()
    ar.release(m0)


def ph_ffn(mk: MK):
    c, ar = mk.ctx, mk.ar
    x3f = mk.stream("x3f", "in")
    x3b = mk.stream("x3b", "in")
    out = mk.hbm("out", (T, D), F32, "out")
    sgT = mk.ht["scrT_wg1"]
    suT = mk.ht["scrT_wu1"]
    S2 = 1.0 / (127.0 * 127.0)
    m0 = ar.mark()
    WD = ar.bbuf(24 * 1024)
    WF = ar.bbuf(4 * 1024)
    XS4 = ar.bbuf(NDC * SUP)
    XF4 = ar.fbuf(NDC * SUP)
    HMALL = ar.bbuf(24 * SUP)
    SGF = ar.bbuf(SUP)
    ZS4 = ar.fbuf(NDC * SUP)
    ZTM = ar.fbuf(4 * D)
    SQT = ar.fbuf(D)
    OUTT = ar.fbuf(D)
    RC4 = ar.fbuf(16)
    PS0, PS1, PS2, PS3 = mk.PS[0], mk.PS[1], mk.PS[2], mk.PS[3]
    wd_scr = mk.ht["scr_wd1"]
    for hb in range(24):
        c.op("sync", lambda s_, hb_=hb: s_.dma_start(
            out=WD[:, hb_ * 1024:(hb_ + 1) * 1024],
            in_=wd_scr[0:D, hb_ * 128:(hb_ + 1) * 128], transpose=True),
            incs=[("L", 16)])
    lwd = c.val("L")
    wfree = [0, 0]
    for sp in range(NSUP):
        t0 = sp * SUP
        xc = x3b[:, t0:t0 + SUP].rearrange("(dc p) t -> p dc t", p=128)
        c.op("sync", lambda s_, i=xc: s_.dma_start(
            out=XS4.rearrange("p (dc t) -> p dc t", dc=NDC), in_=i),
            waits=c.allv("T"), incs=[("L", 16)])
        xcf = x3f[:, t0:t0 + SUP].rearrange("(dc p) t -> p dc t", p=128)
        c.op("sync", lambda s_, i=xcf: s_.dma_start(
            out=XF4.rearrange("p (dc t) -> p dc t", dc=NDC), in_=i),
            waits=c.allv("V"), incs=[("L", 16)])
        lx4 = c.val("L")
        for hb in range(24):
            slot = hb % 2
            wv = [("T", wfree[slot])] if wfree[slot] > 0 else []
            for wi, st in ((0, sgT), (1, suT)):
                c.op("sync", lambda s_, hb_=hb, sl=slot, wi_=wi, st_=st: s_.dma_start(
                    out=WF[:, (sl * 2 + wi_) * 1024:(sl * 2 + wi_ + 1) * 1024].rearrange(
                        "p (dc h) -> p dc h", dc=NDC),
                    in_=st_[:, hb_ * 128:(hb_ + 1) * 128].rearrange("(dc p) h -> p dc h", p=128)),
                    waits=(wv if wi == 0 else ()), incs=[("L", 16)])
            lw4 = c.val("L")
            for gi, ps in ((0, PS0), (1, PS1)):
                for dc in range(NDC):
                    c.op("tensor", lambda t, gi_=gi, dc_=dc, ps_=ps, sl=slot: t.matmul(
                        ps_[:, 0:SUP],
                        WF[:, (sl * 2 + gi_) * 1024 + dc_ * 128:(sl * 2 + gi_) * 1024 + (dc_ + 1) * 128],
                        XS4[:, dc_ * SUP:(dc_ + 1) * SUP],
                        start=(dc_ == 0), stop=(dc_ == NDC - 1), skip_group_check=True),
                        waits=([("L", lw4 if gi == 0 else lx4), ("V", c.val("V")),
                                ("S", c.val("S"))] if dc == 0 else ()),
                        incs=([("T", 1)] if dc == NDC - 1 else ()))
            tgu = c.val("T")
            wfree[slot] = tgu
            c.op("scalar", lambda s_: s_.activation(SGF, PS0[:, 0:SUP], AF.Silu, scale=1.0 / 127.0),
                 waits=[("T", tgu)], incs=[("S", 1)])
            sg4 = c.val("S")
            c.op("vector", lambda v, hb_=hb: v.tensor_mul(
                HMALL[:, hb_ * SUP:(hb_ + 1) * SUP], SGF, PS1[:, 0:SUP]),
                waits=[("S", sg4)], incs=[("V", 1)])
        vhm4 = c.val("V")
        for jb in range(NDC):
            ps = PS2 if (jb % 2 == 0) else PS3
            for hb in range(24):
                c.op("tensor", lambda t, jb_=jb, hb_=hb, ps_=ps: t.matmul(
                    ps_[:, 0:SUP], WD[:, hb_ * 1024 + jb_ * 128: hb_ * 1024 + (jb_ + 1) * 128],
                    HMALL[:, hb_ * SUP:(hb_ + 1) * SUP],
                    start=(hb_ == 0), stop=(hb_ == 23), skip_group_check=True),
                    waits=([("V", vhm4), ("L", lwd)] if (jb == 0 and hb == 0) else
                           ([("V", c.val("V"))] if hb == 0 else ())),
                    incs=([("T", 1)] if hb == 23 else ()))
            td4 = c.val("T")
            c.op("vector", lambda v, jb_=jb, ps_=ps: v.scalar_tensor_tensor(
                ZS4[:, jb_ * SUP:(jb_ + 1) * SUP], ps_[:, 0:SUP], S2,
                XF4[:, jb_ * SUP:(jb_ + 1) * SUP], ALU.mult, ALU.add),
                waits=[("T", td4)], incs=[("V", 1)])
        vz4 = c.val("V")
        for tc in range(NCHS):
            for dc in range(NDC):
                c.op("tensor", lambda t, tc_=tc, dc_=dc: t.transpose(
                    mk.PS[4][:, dc_ * 128:(dc_ + 1) * 128] if dc_ < 4 else
                    mk.PS[5][:, (dc_ - 4) * 128:(dc_ - 3) * 128],
                    ZS4[:, dc_ * SUP + tc_ * 128: dc_ * SUP + (tc_ + 1) * 128], mk.ID_F),
                    waits=([("V", vz4), ("V", c.val("V"))] if dc == 0 else ()),
                    incs=([("T", 1)] if dc == NDC - 1 else ()))
            tz4 = c.val("T")
            c.op("vector", lambda v, tc_=tc: v.tensor_copy(
                ZTM[:, tc_ * D: tc_ * D + 512], mk.PS[4][:, 0:512]),
                waits=[("T", tz4)] + (c.allv("S") if tc == 0 else []), incs=[("V", 1)])
            c.op("vector", lambda v, tc_=tc: v.tensor_copy(
                ZTM[:, tc_ * D + 512: (tc_ + 1) * D], mk.PS[5][:, 0:512]),
                incs=[("V", 1)])
            zt = None
        vzt = c.val("V")
        for tc in range(NCHS):
            ztm = ZTM[:, tc * D:(tc + 1) * D]
            c.op("vector", lambda v, z_=ztm: v.reduce_sum(RC4[:, 0:1], z_, axis=AX.X),
                 waits=[("V", vzt)] if tc == 0 else (), incs=[("V", 1)])
            vs4 = c.val("V")
            c.op("vector", lambda v: v.tensor_scalar(RC4[:, 1:2], RC4[:, 0:1], 1.0 / D, None, ALU.mult),
                 waits=[("V", vs4)], incs=[("V", 1)])
            c.op("scalar", lambda s_, z_=ztm: s_.activation(SQT, z_, AF.Square),
                 waits=[("V", c.val("V"))], incs=[("S", 1)])
            sq4 = c.val("S")
            c.op("vector", lambda v: v.reduce_sum(RC4[:, 2:3], SQT, axis=AX.X),
                 waits=[("S", sq4)], incs=[("V", 1)])
            vq4 = c.val("V")
            c.op("vector", lambda v: v.tensor_scalar(RC4[:, 3:4], RC4[:, 2:3], 1.0 / D, None, ALU.mult),
                 waits=[("V", vq4)], incs=[("V", 1)])
            c.op("vector", lambda v: v.tensor_mul(RC4[:, 6:7], RC4[:, 1:2], RC4[:, 1:2]),
                 incs=[("V", 1)])
            c.op("vector", lambda v: v.scalar_tensor_tensor(
                RC4[:, 7:8], RC4[:, 6:7], -1.0, RC4[:, 3:4], ALU.mult, ALU.add),
                incs=[("V", 1)])
            vv4 = c.val("V")
            c.op("scalar", lambda s_: s_.activation(RC4[:, 8:9], RC4[:, 7:8], AF.Ln,
                                                    bias=mk.ECOLS[:, 1:2]),
                 waits=[("V", vv4)], incs=[("S", 1)])
            c.op("scalar", lambda s_: s_.activation(RC4[:, 9:10], RC4[:, 8:9], AF.Exp, scale=-0.5),
                 incs=[("S", 1)])
            sr4 = c.val("S")
            c.op("vector", lambda v, z_=ztm: v.tensor_scalar(
                OUTT, z_, RC4[:, 1:2], RC4[:, 9:10], ALU.subtract, ALU.mult),
                waits=[("S", sr4)], incs=[("V", 1)])
            vo4 = c.val("V")
            c.op("gpsimd", lambda g, sp_=sp, tc_=tc: g.dma_start(
                out=out[sp_ * SUP + tc_ * 128: sp_ * SUP + (tc_ + 1) * 128, :], in_=OUTT),
                waits=[("V", vo4)], incs=[("G", 16)])
            gof = c.val("G")
            c.op("vector", lambda v, gv_=gof: v.tensor_copy(RC4[:, 10:11], RC4[:, 9:10]),
                 waits=[("G", gof)], incs=[("V", 1)])
    c.barrier()
    ar.release(m0)


def ph_mamba(mk: MK):
    c, ar = mk.ctx, mk.ar
    x2f = mk.stream("x2f", "in")
    x2b = mk.stream("x2b", "in")
    x3f = mk.stream("x3f", "out" if mk.mode != "fused" else "scratch")
    x3b = mk.stream("x3b", "out" if mk.mode != "fused" else "scratch")
    zxb_h = mk.hbm("m_zxb", (4480, T), BF16, "scratch")
    cvs_h = mk.hbm("m_cvs", (NCH, 128, 18 * 128), BF16, "scratch")
    stx_h = mk.hbm("m_stx", (NCH, 128, 2048), BF16, "scratch")
    sth_h = mk.hbm("m_sth", (NCH, 128, 2048), BF16, "scratch")
    rp_h = mk.hbm("m_rp", (NCH, 4, 32, 128), F32, "scratch")
    hx_in = mk.hbm("m_hxin", (128, 54), F32, "scratch")
    hx_out = mk.hbm("m_hxout", (256, 54), F32, "scratch")
    sx_in = mk.hbm("m_sxin", (128, 2048), F32, "scratch")
    sx_out = mk.hbm("m_sxout", (256, 2048), F32, "scratch")
    PS0, PS1, PS2, PS3, PS4, PS5 = mk.PS
    PB0, PB1 = mk.PB

    HALOG = ar.bbuf(64)
    HINIT = ar.bbuf(2048)
    GB = ar.fbuf(1024)
    SCOLS = ar.fbuf(64)
    CBIAS = ar.fbuf(18)
    NWC = ar.fbuf(16)
    DCP = ar.fbuf(16)
    CONVW4 = ar.fbuf(72)

    dtb_in = mk.hbm("i_L1_attn_dt_bias", (M_NH,), F32, "in")
    al_in = mk.hbm("i_L1_attn_A_log", (M_NH,), F32, "in")
    dd_in = mk.hbm("i_L1_attn_D", (M_NH,), F32, "in")
    nw_in = mk.hbm("i_L1_attn_norm_w", (M_DIN,), F32, "in")
    cw_in = mk.hbm("i_L1_attn_conv_w", (M_CONV, 4), F32, "in")
    cb_in = mk.hbm("i_L1_attn_conv_b", (M_CONV,), F32, "in")
    m00 = ar.mark()
    ROWS0 = ar.fbuf(M_CONV, p=1)
    c.op("vector", lambda v: v.memset(SCOLS[:, 2:3], 1.0), incs=[("V", 1)])
    c.op("sync", lambda s_: s_.dma_start(out=ROWS0[0:1, 0:32], in_=dtb_in.unsqueeze(0)), incs=[("L", 16)])
    c.op("sync", lambda s_: s_.dma_start(out=ROWS0[0:1, 32:64], in_=al_in.unsqueeze(0)), incs=[("L", 16)])
    c.op("sync", lambda s_: s_.dma_start(out=ROWS0[0:1, 64:96], in_=dd_in.unsqueeze(0)), incs=[("L", 16)])
    c.op("sync", lambda s_: s_.dma_start(out=ROWS0[0:1, 96:96 + M_DIN], in_=nw_in.unsqueeze(0)), incs=[("L", 16)])
    CWROW = ar.fbuf(9216, p=1)
    c.op("sync", lambda s_: s_.dma_start(
        out=CWROW[0:1, :], in_=cw_in.rearrange("a b -> (a b)").unsqueeze(0)),
        incs=[("L", 16)])
    lcw = c.val("L")
    for jb0 in range(18):
        for k0 in range(4):
            c.op("tensor", lambda t, jb_=jb0, k_=k0: t.matmul(
                PS5[:, jb_ * 4 + k_:jb_ * 4 + k_ + 1],
                CWROW[0:1, jb_ * 512 + k_:jb_ * 512 + k_ + 509:4],
                SCOLS[0:1, 2:3],
                start=(jb_ == 0 and k_ == 0), stop=(jb_ == 17 and k_ == 3),
                skip_group_check=True),
                waits=([("L", lcw), ("V", c.val("V"))] if (jb0 == 0 and k0 == 0) else ()),
                incs=([("T", 1)] if (jb0 == 17 and k0 == 3) else ()))
    tcw = c.val("T")
    c.op("vector", lambda v: v.tensor_copy(CONVW4[:, 0:72], PS5[:, 0:72]),
         waits=[("T", tcw)], incs=[("V", 1)])
    c.op("sync", lambda s_: s_.dma_start(out=ROWS0[0:1, 2176:2176 + M_CONV]
                                         if False else ROWS0[0:1, 0:M_CONV],
                                         in_=cb_in.unsqueeze(0)),
         waits=(), incs=[("L", 16)])
    lv0 = c.val("L")
    ROWS1 = ar.fbuf(M_DIN + 96, p=1)
    c.op("sync", lambda s_: s_.dma_start(out=ROWS1[0:1, 0:32], in_=dtb_in.unsqueeze(0)), incs=[("L", 16)])
    c.op("sync", lambda s_: s_.dma_start(out=ROWS1[0:1, 32:64], in_=al_in.unsqueeze(0)), incs=[("L", 16)])
    c.op("sync", lambda s_: s_.dma_start(out=ROWS1[0:1, 64:96], in_=dd_in.unsqueeze(0)), incs=[("L", 16)])
    c.op("sync", lambda s_: s_.dma_start(out=ROWS1[0:1, 96:96 + M_DIN], in_=nw_in.unsqueeze(0)), incs=[("L", 16)])
    lrows = c.val("L")
    c.op("tensor", lambda t: t.matmul(PS4[0:32, 0:1], ROWS1[0:1, 0:32], SCOLS[0:1, 2:3],
                                      start=True, stop=True, skip_group_check=True),
         waits=[("L", lrows), ("V", c.val("V"))])
    c.op("tensor", lambda t: t.matmul(PS4[0:32, 1:2], ROWS1[0:1, 32:64], SCOLS[0:1, 2:3],
                                      start=True, stop=True, skip_group_check=True))
    for ic in range(16):
        c.op("tensor", lambda t, ic_=ic: t.matmul(
            PS4[:, 2 + ic_:3 + ic_], ROWS1[0:1, 96 + ic_ * 128:96 + (ic_ + 1) * 128],
            SCOLS[0:1, 2:3], start=True, stop=True, skip_group_check=True))
    for jb in range(18):
        c.op("tensor", lambda t, jb_=jb: t.matmul(
            PS4[:, 18 + jb_:19 + jb_], ROWS0[0:1, jb_ * 128:(jb_ + 1) * 128],
            SCOLS[0:1, 2:3], start=True, stop=True, skip_group_check=True),
            waits=([("L", lv0)] if jb == 0 else ()))
    c.op("tensor", lambda t: t.matmul(PS4[0:64, 36:52], mk.ONES1F[0:1, 0:64],
                                      ROWS1[0:1, 64:96:2], start=True, stop=True,
                                      skip_group_check=True))
    c.op("tensor", lambda t: t.matmul(PS4[64:128, 36:52], mk.ONES1F[0:1, 0:64],
                                      ROWS1[0:1, 65:96:2], start=True, stop=True,
                                      skip_group_check=True, tile_position=(0, 64)),
         incs=[("T", 1)])
    tcols = c.val("T")
    c.op("vector", lambda v: v.tensor_copy(SCOLS[:, 0:2], PS4[:, 0:2]), waits=[("T", tcols)], incs=[("V", 1)])
    c.op("vector", lambda v: v.tensor_copy(NWC, PS4[:, 2:18]), incs=[("V", 1)])
    c.op("vector", lambda v: v.tensor_copy(CBIAS[:, 0:18], PS4[:, 18:36]), incs=[("V", 1)])
    c.op("vector", lambda v: v.tensor_copy(DCP, PS4[:, 36:52]), incs=[("V", 1)])
    c.op("scalar", lambda s_: s_.activation(SCOLS[0:32, 1:2], SCOLS[0:32, 1:2], AF.Exp),
         waits=c.allv("V"), incs=[("S", 1)])
    sa = c.val("S")
    c.op("vector", lambda v: v.tensor_scalar(SCOLS[0:32, 1:2], SCOLS[0:32, 1:2], -1.0, None, ALU.mult),
         waits=[("S", sa)], incs=[("V", 1)])
    vcons = c.val("V")
    c.barrier()
    ar.release(m00)

    mA = ar.mark()
    WIP = ar.bbuf(NDC * 4480)
    XSP = ar.bbuf(NDC * SUP)
    ZR = ar.bbuf(2 * SUP)
    HS = ar.bbuf(64)
    load_wT(mk, c, "ipw", D, 4480, WIP)
    lwip = c.val("L")
    zfree = [0, 0]
    for sp in range(NSUP):
        xc = x2b[:, sp * SUP:(sp + 1) * SUP].rearrange("(dc p) t -> p dc t", p=128)
        c.op("sync", lambda s_, i=xc: s_.dma_start(
            out=XSP.rearrange("p (dc t) -> p dc t", dc=NDC), in_=i),
            waits=c.allv("T"), incs=[("L", 16)])
        lxs = c.val("L")
        for jb in range(35):
            ps = PS0 if (jb % 2 == 0) else PS1
            for dc in range(NDC):
                c.op("tensor", lambda t, jb_=jb, dc_=dc, ps_=ps: t.matmul(
                    ps_[:, 0:SUP], WIP[:, dc_ * 4480 + jb_ * 128: dc_ * 4480 + (jb_ + 1) * 128],
                    XSP[:, dc_ * SUP:(dc_ + 1) * SUP],
                    start=(dc_ == 0), stop=(dc_ == NDC - 1), skip_group_check=True),
                    waits=([("L", lxs), ("V", c.val("V"))] if dc == 0 else ()),
                    incs=([("T", 1)] if dc == NDC - 1 else ()))
            tj = c.val("T")
            slot = jb % 2
            zw = [("G", zfree[slot])] if zfree[slot] > 0 else []
            c.op("vector", lambda v, sl=slot, ps_=ps: v.tensor_copy(
                ZR[:, sl * SUP:(sl + 1) * SUP], ps_[:, 0:SUP]),
                waits=[("T", tj)] + zw, incs=[("V", 1)])
            vzr = c.val("V")
            c.op("gpsimd", lambda g, jb_=jb, sp_=sp, sl=slot: g.dma_start(
                out=zxb_h[jb_ * 128:(jb_ + 1) * 128, sp_ * SUP:(sp_ + 1) * SUP],
                in_=ZR[:, sl * SUP:(sl + 1) * SUP]),
                waits=[("V", vzr)], incs=[("G", 16)])
            zfree[slot] = c.val("G")
            if sp == NSUP - 1 and 16 <= jb <= 33:
                c.op("vector", lambda v, jb_=jb, sl=slot: v.tensor_copy(
                    HS[:, (jb_ - 16) * 3:(jb_ - 16) * 3 + 3],
                    ZR[:, sl * SUP + 509: sl * SUP + 512]),
                    incs=[("V", 1)])
    vhalo = c.val("V")
    c.op("gpsimd", lambda g: g.dma_start(out=hx_in, in_=HS[:, 0:54]),
         waits=[("V", vhalo)], incs=[("G", 16)])
    ghx = c.val("G")
    c.op("gpsimd", lambda g: g.collective_compute(
        "AllGather", ALU.bypass, replica_groups=RG_PAIRS,
        ins=[hx_in.opt()], outs=[hx_out.opt()]),
        waits=[("G", ghx)], incs=[("CC", 1)])
    chx = c.val("CC")
    XCH3 = ar.fbuf(64)
    c.op("gpsimd", lambda g: g.dma_start(out=XCH3[:, 0:54], in_=hx_out[0:128, :]),
         waits=[("CC", chx)], incs=[("G", 16)])
    ghx2 = c.val("G")
    c.op("vector", lambda v: v.tensor_scalar(HALOG[:, 0:54], XCH3[:, 0:54],
                                             mk.GATE[:, 0:1], None, ALU.mult),
         waits=[("G", ghx2)], incs=[("V", 1)])
    c.barrier()
    ar.release(mA)

    mC = ar.mark()
    H = ar.fbuf(2048)
    LTOTS = ar.fbuf(32, p=32)
    c.op("vector", lambda v: v.memset(H, 0.0), incs=[("V", 1)])
    c.op("vector", lambda v: v.memset(LTOTS, 0.0), incs=[("V", 1)])
    DT0 = ar.bbuf(128, p=32)
    V1 = ar.fbuf(128, p=32)
    V2 = ar.fbuf(128, p=32)
    V3 = ar.fbuf(128, p=32)
    DTF = ar.fbuf(128, p=32)
    LFM = ar.fbuf(128, p=32)
    LATM = ar.fbuf(32)
    LTM = ar.fbuf(32)
    DTTM = ar.fbuf(32)
    LLB = ar.fbuf(32)
    GREP = ar.fbuf(32)
    VWTM = ar.fbuf(32)
    LLR = ar.fbuf(32, p=1)
    RPS = ar.fbuf(128, p=32)
    CVIN = ar.bbuf(18 * 131)
    CVACC = ar.fbuf(18 * 128)
    TMPC = ar.fbuf(18 * 128)
    CVS = ar.bbuf(18 * 128)
    BTM = ar.bbuf(128)
    XTM = ar.bbuf(2048)
    X2T = ar.bbuf(2048)
    for ch in range(NCH):
        c.op("sync", lambda s_, ch_=ch: s_.dma_start(
            out=DT0[0:32, :], in_=zxb_h[4352:4384, ch_ * CH:(ch_ + 1) * CH]),
            waits=c.allv("V", "T", "S"), incs=[("L", 16)])
        ldt = c.val("L")
        c.op("vector", lambda v: v.tensor_scalar(V1[0:32, :], DT0[0:32, :],
                                                 SCOLS[0:32, 0:1], None, ALU.add),
             waits=[("L", ldt), ("V", vcons)], incs=[("V", 1)])
        c.op("vector", lambda v: v.tensor_scalar(V2[0:32, :], V1[0:32, :], 0.0, None, ALU.max),
             incs=[("V", 1)])
        c.op("vector", lambda v: v.tensor_scalar(V3[0:32, :], V1[0:32, :], 0.0, None, ALU.min),
             incs=[("V", 1)])
        c.op("vector", lambda v: v.tensor_sub(V3[0:32, :], V3[0:32, :], V2[0:32, :]),
             incs=[("V", 1)])
        vsp = c.val("V")
        c.op("scalar", lambda s_: s_.activation(V3[0:32, :], V3[0:32, :], AF.Exp),
             waits=[("V", vsp)], incs=[("S", 1)])
        c.op("scalar", lambda s_: s_.activation(V3[0:32, :], V3[0:32, :], AF.Ln,
                                                bias=SCOLS[0:32, 2:3]), incs=[("S", 1)])
        ssp = c.val("S")
        c.op("vector", lambda v: v.tensor_add(DTF[0:32, :], V3[0:32, :], V2[0:32, :]),
             waits=[("S", ssp)], incs=[("V", 1)])
        c.op("vector", lambda v: v.tensor_scalar(V1[0:32, :], DTF[0:32, :],
                                                 SCOLS[0:32, 1:2], None, ALU.mult),
             incs=[("V", 1)])
        vla = c.val("V")
        c.op("tensor", lambda t: t.transpose(PS4[:, 0:32], V1[0:32, :], mk.ID_F[0:32, 0:32]),
             waits=[("V", vla)])
        c.op("tensor", lambda t: t.transpose(PS4[:, 32:64], DTF[0:32, :], mk.ID_F[0:32, 0:32]),
             incs=[("T", 1)])
        tla = c.val("T")
        c.op("vector", lambda v: v.tensor_copy(LATM, PS4[:, 0:32]), waits=[("T", tla)], incs=[("V", 1)])
        c.op("vector", lambda v: v.tensor_copy(DTTM, PS4[:, 32:64]), incs=[("V", 1)])
        vla2 = c.val("V")
        c.op("tensor", lambda t: t.matmul(PS4[:, 64:96], mk.CUMF, LATM,
                                          start=True, stop=True, skip_group_check=True),
             waits=[("V", vla2)])
        c.op("tensor", lambda t: t.matmul(PS5[0:32, 0:128], LATM, mk.CUMF,
                                          start=True, stop=True, skip_group_check=True),
             incs=[("T", 1)])
        tcs = c.val("T")
        c.op("vector", lambda v: v.tensor_copy(LTM, PS4[:, 64:96]), waits=[("T", tcs)], incs=[("V", 1)])
        c.op("vector", lambda v: v.tensor_copy(LFM[0:32, :], PS5[0:32, 0:128]), incs=[("V", 1)])
        c.op("vector", lambda v, ch_=ch: v.tensor_copy(LTOTS[0:32, ch_:ch_ + 1],
                                                       LFM[0:32, 127:128]), incs=[("V", 1)])
        vlf = c.val("V")
        c.op("tensor", lambda t: t.transpose(PS4[0:1, 96:128], LFM[0:32, 127:128],
                                             mk.ID_F[0:32, 0:32]),
             waits=[("V", vlf)], incs=[("T", 1)])
        tll = c.val("T")
        c.op("vector", lambda v: v.tensor_copy(LLR[0:1, :], PS4[0:1, 96:128]),
             waits=[("T", tll)], incs=[("V", 1)])
        vllr = c.val("V")
        c.op("tensor", lambda t: t.matmul(PS5[:, 128:160], mk.ONES1F[0:1, 0:128], LLR[0:1, :],
                                          start=True, stop=True, skip_group_check=True),
             waits=[("V", vllr)], incs=[("T", 1)])
        tllb = c.val("T")
        c.op("vector", lambda v: v.tensor_copy(LLB, PS5[:, 128:160]), waits=[("T", tllb)], incs=[("V", 1)])
        vllb = c.val("V")
        c.op("scalar", lambda s_: s_.activation(GREP, LLB, AF.Exp), waits=[("V", vllb)], incs=[("S", 1)])
        c.op("vector", lambda v: v.tensor_sub(VWTM, LLB, LTM), incs=[("V", 1)])
        vvw = c.val("V")
        c.op("scalar", lambda s_: s_.activation(VWTM, VWTM, AF.Exp), waits=[("V", vvw)], incs=[("S", 1)])
        svw = c.val("S")
        c.op("vector", lambda v: v.tensor_mul(VWTM, VWTM, DTTM), waits=[("S", svw)], incs=[("V", 1)])
        c.op("gpsimd", lambda g, ch_=ch: g.dma_start(out=rp_h[ch_, 0, :, :], in_=LFM[0:32, :]),
             waits=[("V", c.val("V"))], incs=[("G", 16)])
        c.op("vector", lambda v: v.tensor_scalar(RPS[0:32, :], LFM[0:32, :], -1.0, None, ALU.mult),
             incs=[("V", 1)])
        vneg = c.val("V")
        c.op("gpsimd", lambda g, ch_=ch: g.dma_start(out=rp_h[ch_, 1, :, :], in_=RPS[0:32, :]),
             waits=[("V", vneg)], incs=[("G", 16)])
        c.op("scalar", lambda s_: s_.activation(V2[0:32, :], DTF[0:32, :], AF.Ln),
             incs=[("S", 1)])
        sln = c.val("S")
        c.op("gpsimd", lambda g, ch_=ch: g.dma_start(out=rp_h[ch_, 2, :, :], in_=V2[0:32, :]),
             waits=[("S", sln)], incs=[("G", 16)])
        c.op("scalar", lambda s_: s_.activation(V3[0:32, :], LFM[0:32, :], AF.Exp),
             incs=[("S", 1)])
        sexp = c.val("S")
        c.op("gpsimd", lambda g, ch_=ch: g.dma_start(out=rp_h[ch_, 3, :, :], in_=V3[0:32, :]),
             waits=[("S", sexp)], incs=[("G", 16)])
        if ch == 0:
            c.op("sync", lambda s_: s_.dma_start(
                out=CVIN[:, 0:2358].rearrange("p (jb t) -> p jb t", jb=18)[:, :, 3:131],
                in_=zxb_h[2048:4352, 0:CH].rearrange("(jb p) t -> p jb t", p=128)),
                incs=[("L", 16)])
            lcv = c.val("L")
            c.op("vector", lambda v: v.tensor_copy(
                CVIN[:, 0:2358].rearrange("p (jb t) -> p jb t", jb=18)[:, :, 0:3],
                HALOG[:, 0:54].rearrange("p (jb t) -> p jb t", jb=18)),
                waits=[("L", lcv)], incs=[("V", 1)])
        else:
            c.op("sync", lambda s_, ch_=ch: s_.dma_start(
                out=CVIN[:, 0:2358].rearrange("p (jb t) -> p jb t", jb=18),
                in_=zxb_h[2048:4352, ch_ * CH - 3: (ch_ + 1) * CH].rearrange(
                    "(jb p) t -> p jb t", p=128)),
                waits=c.allv("V", "S"), incs=[("L", 16)])
        lcv = c.val("L")
        cv3 = CVIN[:, 0:2358].rearrange("p (jb t) -> p jb t", jb=18)
        w4 = CONVW4[:, 0:72].rearrange("p (jb k) -> p k jb", k=4)
        c.op("vector", lambda v: v.tensor_tensor(
            CVACC.rearrange("p (jb t) -> p jb t", jb=18), cv3[:, :, 0:128],
            w4[:, 0, :].unsqueeze(2).broadcast_to((128, 18, 128)), ALU.mult),
            waits=[("L", lcv)] + c.allv("S"))
        for k in range(1, 4):
            c.op("vector", lambda v, k_=k: v.tensor_tensor(
                TMPC.rearrange("p (jb t) -> p jb t", jb=18), cv3[:, :, k_:k_ + 128],
                w4[:, k_, :].unsqueeze(2).broadcast_to((128, 18, 128)), ALU.mult))
            c.op("vector", lambda v: v.tensor_tensor(CVACC, CVACC, TMPC, ALU.add))
        c.op("vector", lambda v: v.tensor_tensor(
            CVACC.rearrange("p (jb t) -> p jb t", jb=18),
            CVACC.rearrange("p (jb t) -> p jb t", jb=18),
            CBIAS[:, 0:18].unsqueeze(2).broadcast_to((128, 18, 128)), ALU.add),
            incs=[("V", 1)])
        pcv = c.val("V")
        c.op("scalar", lambda s_: s_.activation(CVS, CVACC, AF.Silu),
             waits=[("V", pcv)] + c.allv("T", "G"), incs=[("S", 1)])
        scv = c.val("S")
        c.op("gpsimd", lambda g, ch_=ch: g.dma_start(out=cvs_h[ch_, :, :], in_=CVS),
             waits=[("S", scv)], incs=[("G", 16)])
        c.op("tensor", lambda t: t.transpose(PB0[:, 0:128], CVS[:, 16 * 128:17 * 128], mk.ID_B),
             waits=[("S", scv), ("V", c.val("V"))], incs=[("T", 1)])
        tbt = c.val("T")
        c.op("vector", lambda v: v.tensor_copy(BTM, PB0[:, 0:128]), waits=[("T", tbt)], incs=[("V", 1)])
        for half in range(2):
            for i in range(8):
                c.op("tensor", lambda t, hf=half, i_=i: t.transpose(
                    PB1[:, i_ * 128:(i_ + 1) * 128],
                    CVS[:, (hf * 8 + i_) * 128:(hf * 8 + i_ + 1) * 128], mk.ID_B),
                    waits=([("V", c.val("V"))] if i == 0 else ()),
                    incs=([("T", 1)] if i == 7 else ()))
            txt = c.val("T")
            c.op("vector", lambda v, hf=half: v.tensor_copy(
                XTM[:, hf * 1024:(hf + 1) * 1024], PB1[:, 0:1024]),
                waits=[("T", txt)] + (c.allv("G") if half == 0 else []), incs=[("V", 1)])
        vxt = c.val("V")
        c.op("gpsimd", lambda g, ch_=ch: g.dma_start(out=stx_h[ch_, :, :], in_=XTM),
             waits=[("V", vxt)], incs=[("G", 16)])
        c.op("vector", lambda v: v.tensor_tensor(
            X2T.rearrange("p (h e) -> p h e", h=32),
            XTM.rearrange("p (h e) -> p h e", h=32),
            VWTM.unsqueeze(2).broadcast_to((128, 32, 64)), ALU.mult),
            incs=[("V", 1)])
        vx2 = c.val("V")
        c.op("gpsimd", lambda g, ch_=ch: g.dma_start(out=sth_h[ch_, :, :], in_=H),
             waits=c.allv("V"), incs=[("G", 16)])
        gsth = c.val("G")
        for h in range(M_NH):
            q = h // 8
            psq = mk.PS[q]
            c.op("tensor", lambda t, h_=h, psq_=psq: t.matmul(
                psq_[:, (h_ % 8) * 64:(h_ % 8 + 1) * 64], BTM, X2T[:, h_ * 64:(h_ + 1) * 64],
                start=(h_ % 8 == 0), stop=True, skip_group_check=True),
                waits=([("V", vx2)] if h == 0 else ()),
                incs=([("T", 1)] if h == M_NH - 1 else ()))
        tctr = c.val("T")
        c.op("vector", lambda v: v.tensor_tensor(
            H.rearrange("p (h e) -> p h e", h=32),
            H.rearrange("p (h e) -> p h e", h=32),
            GREP.unsqueeze(2).broadcast_to((128, 32, 64)), ALU.mult),
            waits=[("G", gsth), ("T", tctr)] + c.allv("S"), incs=[("V", 1)])
        for q in range(4):
            c.op("vector", lambda v, q_=q: v.tensor_add(
                H[:, q_ * 512:(q_ + 1) * 512], H[:, q_ * 512:(q_ + 1) * 512],
                mk.PS[q_][:, 0:512]), incs=[("V", 1)])
    vhf = c.val("V")
    c.op("gpsimd", lambda g: g.dma_start(out=sx_in, in_=H), waits=[("V", vhf)], incs=[("G", 16)])
    gsx = c.val("G")
    c.op("gpsimd", lambda g: g.collective_compute(
        "AllGather", ALU.bypass, replica_groups=RG_PAIRS,
        ins=[sx_in.opt()], outs=[sx_out.opt()]),
        waits=[("G", gsx)], incs=[("CC", 1)])
    csx = c.val("CC")
    XCH4 = ar.fbuf(2048)
    c.op("gpsimd", lambda g: g.dma_start(out=XCH4, in_=sx_out[0:128, :]),
         waits=[("CC", csx)], incs=[("G", 16)])
    gsx2 = c.val("G")
    c.op("vector", lambda v: v.tensor_scalar(HINIT, XCH4, mk.GATE[:, 0:1], None, ALU.mult),
         waits=[("G", gsx2)], incs=[("V", 1)])
    c.op("tensor", lambda t: t.transpose(PS4[0:32, 0:32], LTOTS[0:32, :], mk.ID_F[0:32, 0:32]),
         waits=[("V", c.val("V"))], incs=[("T", 1)])
    tlt = c.val("T")
    c.op("vector", lambda v: v.tensor_copy(V1[0:32, 0:32], PS4[0:32, 0:32]),
         waits=[("T", tlt)], incs=[("V", 1)])
    vlt = c.val("V")
    c.op("tensor", lambda t: t.matmul(PS4[0:32, 32:64], V1[0:32, 0:32], mk.TRILX,
                                      start=True, stop=True, skip_group_check=True),
         waits=[("V", vlt)], incs=[("T", 1)])
    tgp = c.val("T")
    c.op("vector", lambda v: v.tensor_copy(V2[0:32, 0:32], PS4[0:32, 32:64]),
         waits=[("T", tgp)], incs=[("V", 1)])
    vgp = c.val("V")
    c.op("scalar", lambda s_: s_.activation(V2[0:32, 0:32], V2[0:32, 0:32], AF.Exp),
         waits=[("V", vgp)], incs=[("S", 1)])
    sgp = c.val("S")
    gpf_h = mk.hbm("m_gpf", (32, 32), F32, "scratch")
    c.op("gpsimd", lambda g: g.dma_start(out=gpf_h, in_=V2[0:32, 0:32]),
         waits=[("S", sgp)], incs=[("G", 16)])
    ggp = c.val("G")
    GROW = ar.fbuf(1024, p=1)
    c.op("sync", lambda s_: s_.dma_start(
        out=GROW[0:1, :], in_=gpf_h.rearrange("a b -> (a b)").unsqueeze(0)),
        waits=[("G", ggp)], incs=[("L", 16)])
    lgr = c.val("L")
    for i2 in range(2):
        c.op("tensor", lambda t, i2_=i2: t.matmul(
            PS4[:, 0:512] if i2_ == 0 else PS5[:, 0:512], mk.ONES1F[0:1, 0:128],
            GROW[0:1, i2_ * 512:(i2_ + 1) * 512],
            start=True, stop=True, skip_group_check=True),
            waits=([("L", lgr), ("V", c.val("V"))] if i2 == 0 else ()),
            incs=([("T", 1)] if i2 == 1 else ()))
    tgb = c.val("T")
    c.op("vector", lambda v: v.tensor_copy(GB[:, 0:512], PS4[:, 0:512]),
         waits=[("T", tgb)], incs=[("V", 1)])
    c.op("vector", lambda v: v.tensor_copy(GB[:, 512:1024], PS5[:, 0:512]), incs=[("V", 1)])
    c.barrier()
    ar.release(mC)

    mE = ar.mark()
    OPWT = ar.bbuf(16 * 1024)
    T3M = ar.fbuf(4096, p=65)
    R3M = ar.fbuf(4096, p=3)
    HST = ar.bbuf(2048)
    XTM2 = ar.bbuf(2048)
    CVS2 = ar.bbuf(18 * 128)
    ZCH = ar.bbuf(2048)
    HTR = ar.bbuf(2048)
    EBC = ar.bbuf(4096)
    CPALL = ar.bbuf(4096)
    MTALL = ar.bbuf(4096)
    SM3 = ar.bbuf(128)
    GEX = ar.bbuf(1024)
    SCR3 = ar.fbuf(1024)
    YSB = ar.fbuf(2048)
    SZ3 = ar.fbuf(2048)
    SQ3 = ar.bbuf(2048)
    GN4 = ar.bbuf(16 * SUP)
    RC3 = ar.fbuf(16)
    RROW = ar.fbuf(128, p=1)
    RBC = ar.fbuf(128)
    UF3 = ar.fbuf(NDC * SUP)
    ZS3 = UF3
    mk.LN_ROWS = T3M[0:1, 0:6 * SUP]
    mk.LN_ABC = SCR3
    LNB3 = ar.bbuf(NDC * SUP)
    LZB3 = ar.bbuf(NDC * SUP)
    mk.LN_ZB = LZB3
    mk.LN_SQ = LNB3
    scr_opw = mk.ht["scr_opw"]
    for ic in range(16):
        c.op("sync", lambda s_, ic_=ic: s_.dma_start(
            out=OPWT[:, ic_ * 1024:(ic_ + 1) * 1024],
            in_=scr_opw[0:D, ic_ * 128:(ic_ + 1) * 128], transpose=True),
            incs=[("L", 16)])
    lop = c.val("L")
    ones4k = mk.inline("c_ones4k", np.ones((1, 4096), np.float32))
    c.op("sync", lambda s_: s_.dma_start(out=T3M[1:2, :], in_=ones4k), incs=[("L", 16)])
    c.op("sync", lambda s_: s_.dma_start(out=R3M[0:1, :], in_=ones4k), incs=[("L", 16)])
    c.op("sync", lambda s_: s_.dma_start(out=R3M[2:3, :], in_=ones4k), incs=[("L", 16)])
    vmr = c.val("V")
    for ch in range(NCH):
        grp = ch % NCHS
        c.op("sync", lambda s_, ch_=ch: s_.dma_start(out=HST, in_=sth_h[ch_, :, :]),
             waits=c.allv("T", "V"), incs=[("L", 16)])
        c.op("sync", lambda s_, ch_=ch: s_.dma_start(out=XTM2, in_=stx_h[ch_, :, :]),
             incs=[("L", 16)])
        c.op("sync", lambda s_, ch_=ch: s_.dma_start(out=CVS2, in_=cvs_h[ch_, :, :]),
             incs=[("L", 16)])
        c.op("sync", lambda s_, ch_=ch: s_.dma_start(
            out=ZCH.rearrange("p (jb t) -> p jb t", jb=16),
            in_=zxb_h[0:2048, ch_ * CH:(ch_ + 1) * CH].rearrange("(jb p) t -> p jb t", p=128)),
            incs=[("L", 16)])
        for buf, sl, rp in ((T3M, 0, 1), (T3M, 2, 2), (R3M, 1, 0), (T3M, 64, 3)):
            c.op("sync", lambda s_, b_=buf, sl_=sl, rp_=rp, ch_=ch: s_.dma_start(
                out=b_[sl_:sl_ + 1, :],
                in_=rp_h[ch_, rp_].rearrange("a b -> (a b)").unsqueeze(0)),
                incs=[("L", 16)])
        lch = c.val("L")
        for i2 in range(8):
            c.op("tensor", lambda t, i2_=i2: t.matmul(
                PS5[:, 0:512], mk.ONES1F[64:65, 0:128], T3M[64:65, i2_ * 512:(i2_ + 1) * 512],
                start=True, stop=True, skip_group_check=True),
                waits=[("L", lch), ("V", c.val("V"))], incs=[("T", 1)])
            te = c.val("T")
            c.op("vector", lambda v, i2_=i2: v.tensor_copy(
                EBC[:, i2_ * 512:(i2_ + 1) * 512], PS5[:, 0:512]),
                waits=[("T", te)], incs=[("V", 1)])
        vebc = c.val("V")
        c.op("vector", lambda v: v.tensor_tensor(
            CPALL.rearrange("p (h t) -> p h t", h=32),
            CVS2[:, 17 * 128:18 * 128].unsqueeze(1).broadcast_to((128, 32, 128)),
            EBC.rearrange("p (h t) -> p h t", h=32), ALU.mult),
            waits=[("V", vebc)], incs=[("V", 1)])
        c.op("tensor", lambda t: t.matmul(PS4[:, 0:128], CVS2[:, 16 * 128:17 * 128],
                                          CVS2[:, 17 * 128:18 * 128],
                                          start=True, stop=True, skip_group_check=True),
             waits=[("L", lch), ("V", c.val("V"))], incs=[("T", 1)])
        tsb = c.val("T")
        c.op("vector", lambda v: v.tensor_tensor(SM3, PS4[:, 0:128], mk.MASK, ALU.mult),
             waits=[("T", tsb)], incs=[("V", 1)])
        vsm3 = c.val("V")
        gbv = GB.rearrange("p (h cc) -> p h cc", h=32)[:, :, ch:ch + 1]
        c.op("vector", lambda v, g_=gbv: v.tensor_tensor(
            HTR.rearrange("p (h e) -> p h e", h=32),
            HINIT.rearrange("p (h e) -> p h e", h=32),
            g_.broadcast_to((128, 32, 64)), ALU.mult),
            incs=[("V", 1)])
        c.op("vector", lambda v: v.tensor_add(HTR, HTR, HST), incs=[("V", 1)])
        vhtr = c.val("V")
        for b8 in range(4):
            for hh in range(8):
                h = b8 * 8 + hh
                ps = PS0 if hh < 4 else PS1
                c.op("tensor", lambda t, h_=h, hh_=hh, ps_=ps: t.matmul(
                    ps_[:, (hh_ % 4) * 128:(hh_ % 4 + 1) * 128],
                    T3M[0:3, h_ * 128:(h_ + 1) * 128],
                    R3M[0:3, h_ * 128:(h_ + 1) * 128],
                    start=True, stop=True, skip_group_check=True),
                    waits=([("V", c.val("V"))] if hh == 0 else ()),
                    incs=([("T", 1)] if hh == 7 else ()))
            td8 = c.val("T")
            for pi, ps in ((0, PS0), (1, PS1)):
                c.op("vector", lambda v, pi_=pi, ps_=ps: v.tensor_scalar(
                    SCR3[:, pi_ * 512:(pi_ + 1) * 512], ps_[:, 0:512], 30.0, None, ALU.min),
                    waits=[("T", td8)] + c.allv("P"), incs=[("V", 1)])
            vd8 = c.val("V")
            c.op("scalar", lambda s_: s_.activation(GEX, SCR3, AF.Exp),
                 waits=[("V", vd8)], incs=[("S", 1)])
            sd8 = c.val("S")
            c.op("vector", lambda v, b8_=b8: v.tensor_tensor(
                MTALL.rearrange("p (h t) -> p h t", h=32)[:, b8_ * 8:(b8_ + 1) * 8, :],
                GEX.rearrange("p (h t) -> p h t", h=8),
                SM3.unsqueeze(1).broadcast_to((128, 8, 128)), ALU.mult),
                waits=[("S", sd8), ("V", vsm3)], incs=[("V", 1)])
        vmt = c.val("V")
        if "dbg_mt" in mk.taps and ch == 0:
            dmt = mk.hbm("dbg_mt", (7, 128, 4096), F32, "out")
            for si, buf in ((3, SM3), (4, T3M[0:3, :]), (5, R3M[0:3, :]), (6, GEX)):
                p_, f_ = buf.shape[0], buf.shape[1]
                c.op("gpsimd", lambda g, si_=si, b_=buf, p2=p_, f2=f_:
                     g.dma_start(out=dmt[si_, 0:p2, 0:f2], in_=b_),
                     waits=[("V", vmt)], incs=[("G", 16)])
            for si, buf in ((0, MTALL), (1, EBC), (2, CPALL)):
                c.op("gpsimd", lambda g, si_=si, b_=buf: g.dma_start(out=dmt[si_, :, :], in_=b_),
                     waits=[("V", vmt)], incs=[("G", 16)])
            gmt = c.val("G")
            c.op("vector", lambda v: v.tensor_copy(RC3[:, 9:10], RC3[:, 9:10]),
                 waits=[("G", gmt)], incs=[("V", 1)])
        for h in range(M_NH):
            q = h // 8
            j = (h % 8) // 2
            ro = 64 * (h % 2)
            tp = (0, 64) if (h % 2) else None
            st = (h % 8) in (0, 1)
            c.op("tensor", lambda t, h_=h, q_=q, j_=j, tp_=tp, st_=st: t.matmul(
                mk.PS[q_][0:64 if tp_ is None else 64, j_ * 128:(j_ + 1) * 128]
                if tp_ is None else mk.PS[q_][64:128, j_ * 128:(j_ + 1) * 128],
                XTM2[:, h_ * 64:(h_ + 1) * 64], MTALL[:, h_ * 128:(h_ + 1) * 128],
                start=st_, stop=False, skip_group_check=True, tile_position=tp_),
                waits=([("V", vmt), ("V", vhtr)] if h == 0 else ()))
            c.op("tensor", lambda t, h_=h, q_=q, j_=j, tp_=tp: t.matmul(
                mk.PS[q_][0:64, j_ * 128:(j_ + 1) * 128]
                if tp_ is None else mk.PS[q_][64:128, j_ * 128:(j_ + 1) * 128],
                HTR[:, h_ * 64:(h_ + 1) * 64], CPALL[:, h_ * 128:(h_ + 1) * 128],
                start=False, stop=True, skip_group_check=True, tile_position=tp_),
                incs=([("T", 1)] if h == M_NH - 1 else ()))
        ty = c.val("T")
        for q in range(4):
            for rh in range(2):
                c.op("vector", lambda v, q_=q, rh_=rh: v.tensor_copy(
                    YSB.rearrange("p (dc t) -> p dc t", dc=16)[
                        rh_ * 64:rh_ * 64 + 64, 4 * q_:4 * q_ + 4, :],
                    mk.PS[q_][rh_ * 64:rh_ * 64 + 64, 0:512].rearrange(
                        "p (j t) -> p j t", j=4)),
                    waits=([("T", ty)] if (q == 0 and rh == 0) else ()),
                    incs=[("V", 1)])
        if "dbg_y" in mk.taps:
            dyh = mk.hbm("dbg_y", (NCH, 128, 2048), F32, "out")
            vty = c.val("V")
            c.op("gpsimd", lambda g, ch_=ch: g.dma_start(out=dyh[ch_, :, :], in_=YSB),
                 waits=[("V", vty)], incs=[("G", 16)])
            gdy = c.val("G")
            c.op("vector", lambda v: v.tensor_copy(RC3[:, 8:9], RC3[:, 8:9]),
                 waits=[("G", gdy)], incs=[("V", 1)])
        c.op("vector", lambda v: v.tensor_tensor(
            SZ3.rearrange("p (dc t) -> p dc t", dc=16),
            CVS2[:, 0:2048].rearrange("p (dc t) -> p dc t", dc=16),
            DCP.unsqueeze(2).broadcast_to((128, 16, 128)), ALU.mult),
            incs=[("V", 1)])
        c.op("vector", lambda v: v.tensor_add(YSB, YSB, SZ3), incs=[("V", 1)])
        vy3 = c.val("V")
        if "dbg_tl" in mk.taps and ch == 0:
            dtl = mk.hbm("dbg_tl", (6, 128, 2048), F32, "out")
            c.op("gpsimd", lambda g: g.dma_start(out=dtl[0, :, :], in_=SZ3),
                 waits=[("V", vy3)], incs=[("G", 16)])
            c.op("gpsimd", lambda g: g.dma_start(out=dtl[1, :, :], in_=YSB),
                 incs=[("G", 16)])
            gt0 = c.val("G")
            c.op("vector", lambda v: v.tensor_copy(RC3[:, 5:6], RC3[:, 5:6]),
                 waits=[("G", gt0)], incs=[("V", 1)])
            vy3 = c.val("V")
        c.op("scalar", lambda s_: s_.activation(SZ3, ZCH, AF.Silu),
             waits=[("V", vy3)], incs=[("S", 1)])
        ssz = c.val("S")
        if "dbg_tl" in mk.taps and ch == 0:
            dtl = mk.ht["dbg_tl"]
            c.op("gpsimd", lambda g: g.dma_start(out=dtl[2, :, :], in_=SZ3),
                 waits=[("S", ssz)], incs=[("G", 16)])
        c.op("vector", lambda v: v.tensor_mul(YSB, YSB, SZ3), waits=[("S", ssz)], incs=[("V", 1)])
        vg3 = c.val("V")
        if "dbg_tl" in mk.taps and ch == 0:
            dtl = mk.ht["dbg_tl"]
            c.op("gpsimd", lambda g: g.dma_start(out=dtl[3, :, :], in_=YSB),
                 waits=[("V", vg3)], incs=[("G", 16)])
            gt1 = c.val("G")
            c.op("vector", lambda v: v.tensor_copy(RC3[:, 5:6], RC3[:, 5:6]),
                 waits=[("G", gt1)], incs=[("V", 1)])
            vg3 = c.val("V")
        c.op("scalar", lambda s_: s_.activation(SQ3, YSB, AF.Square),
             waits=[("V", vg3)], incs=[("S", 1)])
        sq3 = c.val("S")
        for dc in range(16):
            c.op("tensor", lambda t, dc_=dc: t.matmul(
                PS4[0:1, 256:384], mk.ONESCB[:, 0:1], SQ3[:, dc_ * 128:(dc_ + 1) * 128],
                start=(dc_ == 0), stop=(dc_ == 15), skip_group_check=True),
                waits=([("S", sq3)] if dc == 0 else ()),
                incs=([("T", 1)] if dc == 15 else ()))
        tsq = c.val("T")
        c.op("vector", lambda v: v.tensor_scalar(RROW[0:1, :], PS4[0:1, 256:384],
                                                 1.0 / M_DIN, None, ALU.mult),
             waits=[("T", tsq)], incs=[("V", 1)])
        vr3 = c.val("V")
        c.op("scalar", lambda s_: s_.activation(RROW[0:1, :], RROW[0:1, :], AF.Ln,
                                                bias=mk.ECOLS[0:1, 1:2]),
             waits=[("V", vr3)], incs=[("S", 1)])
        c.op("scalar", lambda s_: s_.activation(RROW[0:1, :], RROW[0:1, :], AF.Exp, scale=-0.5),
             incs=[("S", 1)])
        sr3 = c.val("S")
        c.op("tensor", lambda t: t.matmul(PS4[:, 384:512], mk.ONES1F[0:1, 0:128], RROW[0:1, :],
                                          start=True, stop=True, skip_group_check=True),
             waits=[("S", sr3), ("V", c.val("V"))], incs=[("T", 1)])
        trb = c.val("T")
        c.op("vector", lambda v: v.tensor_copy(RBC, PS4[:, 384:512]),
             waits=[("T", trb)], incs=[("V", 1)])
        vrb = c.val("V")
        if "dbg_tl" in mk.taps and ch == 0:
            dtl = mk.ht["dbg_tl"]
            c.op("gpsimd", lambda g: g.dma_start(out=dtl[4, :, :], in_=SQ3),
                 waits=[("T", tsq)], incs=[("G", 16)])
            c.op("gpsimd", lambda g: g.dma_start(out=dtl[5, :, 0:128], in_=RBC),
                 waits=[("V", vrb)], incs=[("G", 16)])
            gt2 = c.val("G")
            c.op("vector", lambda v: v.tensor_copy(RC3[:, 5:6], RC3[:, 5:6]),
                 waits=[("G", gt2)], incs=[("V", 1)])
            vrb = c.val("V")
        c.op("vector", lambda v: v.tensor_tensor(
            YSB.rearrange("p (dc t) -> p dc t", dc=16), YSB.rearrange("p (dc t) -> p dc t", dc=16),
            NWC.unsqueeze(2).broadcast_to((128, 16, 128)), ALU.mult),
            waits=[("V", vrb)], incs=[("V", 1)])
        c.op("vector", lambda v, g_=grp: v.tensor_tensor(
            GN4.rearrange("p (dc t) -> p dc t", dc=16)[:, :, g_ * 128:(g_ + 1) * 128],
            YSB.rearrange("p (dc t) -> p dc t", dc=16),
            RBC.unsqueeze(1).broadcast_to((128, 16, 128)), ALU.mult),
            incs=[("V", 1)])
        if grp == NCHS - 1:
            sp = ch // NCHS
            vgn = c.val("V")
            if "dbg_gn" in mk.taps:
                dgn = (mk.hbm("dbg_gn", (NSUP, 128, 16 * SUP), F32, "out")
                       if sp == 0 else mk.ht["dbg_gn"])
                c.op("gpsimd", lambda g, sp_=sp, d_=dgn: g.dma_start(
                    out=d_[sp_, :, :], in_=GN4), waits=[("V", vgn)], incs=[("G", 16)])
                ggn = c.val("G")
                c.op("vector", lambda v: v.tensor_copy(RC3[:, 7:8], RC3[:, 7:8]),
                     waits=[("G", ggn)], incs=[("V", 1)])
            uc = x2f[:, sp * SUP:(sp + 1) * SUP].rearrange("(dc p) t -> p dc t", p=128)
            c.op("sync", lambda s_, i=uc: s_.dma_start(
                out=UF3.rearrange("p (dc t) -> p dc t", dc=NDC), in_=i),
                waits=c.allv("V"), incs=[("L", 16)])
            lu3 = c.val("L")
            for jb in range(NDC):
                ps = PS4 if (jb % 2 == 0) else PS5
                for ic in range(16):
                    c.op("tensor", lambda t, jb_=jb, ic_=ic, ps_=ps: t.matmul(
                        ps_[:, 0:SUP],
                        OPWT[:, ic_ * 1024 + jb_ * 128: ic_ * 1024 + (jb_ + 1) * 128],
                        GN4[:, ic_ * SUP:(ic_ + 1) * SUP],
                        start=(ic_ == 0), stop=(ic_ == 15), skip_group_check=True),
                        waits=([("V", vgn), ("L", lop), ("V", c.val("V"))] if (jb == 0 and ic == 0)
                               else ([("V", c.val("V"))] if ic == 0 else ())),
                        incs=([("T", 1)] if ic == 15 else ()))
                top = c.val("T")
                c.op("vector", lambda v, jb_=jb, ps_=ps: v.scalar_tensor_tensor(
                    UF3[:, jb_ * SUP:(jb_ + 1) * SUP], UF3[:, jb_ * SUP:(jb_ + 1) * SUP],
                    2.0, ps_[:, 0:SUP], ALU.mult, ALU.add),
                    waits=[("T", top), ("L", lu3)] + (c.allv("P") if jb == 0 else []),
                    incs=[("V", 1)])
            if "dbg_uf" in mk.taps:
                vuf = c.val("V")
                duf = (mk.hbm("dbg_uf", (NSUP, 128, NDC * SUP), F32, "out")
                       if sp == 0 else mk.ht["dbg_uf"])
                c.op("gpsimd", lambda g, sp_=sp, d_=duf: g.dma_start(
                    out=d_[sp_, :, :], in_=UF3), waits=[("V", vuf)], incs=[("G", 16)])
                guf = c.val("G")
                c.op("vector", lambda v: v.tensor_copy(RC3[:, 6:7], RC3[:, 6:7]),
                     waits=[("G", guf)], incs=[("V", 1)])
            pd3 = ln_fm(mk, c, lambda dc: ZS3[:, dc * SUP:(dc + 1) * SUP], SUP,
                        lambda dc: LNB3[:, dc * SUP:(dc + 1) * SUP],
                        mk.PS[2], mk.PS[3], mk.PS[4])
            for dst, src_ in ((x3f, ZS3), (x3b, LNB3)):
                c.op("gpsimd", lambda g, d_=dst, s_=src_, sp_=sp: g.dma_start(
                    out=d_[:, sp_ * SUP:(sp_ + 1) * SUP].rearrange("(dc p) t -> p dc t", p=128),
                    in_=s_.rearrange("p (dc t) -> p dc t", dc=NDC)),
                    waits=[("P", pd3)], incs=[("G", 16)])
    c.barrier()
    ar.release(mE)


def build(mode="fused", taps=()):
    mk = MK(mode, taps)
    ph_consts(mk)
    if mode in ("fused", "g1"):
        ph_prep(mk, GROUP_W["g1"])
        ph_l0(mk)
    if mode in ("fused", "g2"):
        ph_prep(mk, GROUP_W["g2"])
        ph_router(mk)
        ph_moe(mk)
    if mode in ("fused", "g3"):
        ph_prep(mk, GROUP_W["g3"])
        ph_mamba(mk)
    if mode in ("fused", "g4"):
        ph_prep(mk, GROUP_W["g4"])
        ph_prep_ffnT(mk)
        ph_ffn(mk)
    mk.ctx.run()
    return mk


def shard_inputs(inputs, needed):
    x = np.asarray(inputs["L0_attn_x"])
    maps = []
    for core in range(8):
        b, hh = core // 2, core % 2
        m = {}
        if "x0f" in needed:
            m["x0f"] = np.ascontiguousarray(x[b, hh * T:(hh + 1) * T, :].T)
        m["gate"] = np.full((128, 1), float(hh), np.float32)
        for nm, arr in inputs.items():
            key = "i_" + nm
            if key in needed:
                m[key] = np.ascontiguousarray(arr)
        maps.append({k: v for k, v in m.items() if k in needed})
    return maps


def kernel(**inputs):
    out = np.zeros((4, 8192, 1024), np.float32)
    streams = {}
    for mode in ("g1", "g2", "g3", "g4"):
        mk = build(mode)
        needed = {name for name, k in mk.htkind.items() if k == "in"}
        in_maps = shard_inputs(inputs, needed)
        for core in range(8):
            for nm, arrs in streams.items():
                if nm in needed:
                    in_maps[core][nm] = arrs[core]
        res = run_bass_kernel_spmd(mk.nc, in_maps, core_ids=list(range(8)))
        streams = {name: [np.asarray(res.results[c][name]) for c in range(8)]
                   for name, kind in mk.htkind.items()
                   if kind == "out" and name != "out"}
    for core in range(8):
        b, hh = core // 2, core % 2
        out[b, hh * T:(hh + 1) * T, :] = np.asarray(res.results[core]["out"])
    return out
